# version 32
# baseline (speedup 1.0000x reference)
import numpy as np
import ml_dtypes
from contextlib import ExitStack

import concourse.bass as bass
import concourse.mybir as mybir
from concourse.bass_utils import run_bass_kernel_spmd

BF = ml_dtypes.bfloat16
F32 = mybir.dt.float32
BF16 = mybir.dt.bfloat16
AF = mybir.ActivationFunctionType
ALU = mybir.AluOpType

D = 1024
L = 4096
NS = 8
CH = 512
NCH = 8
H = 16
DK = 64
PM = 256
INTER = 4096
PSCALE = float(1.0 / (64.0 ** 0.25))
LNEPS = float(np.log(np.float32(1e-6)))

ENGINES = ("tensor", "vector", "scalar", "gpsimd", "sync")


class Sem:
    def __init__(self, handle):
        self.h = handle
        self.n = 0


class Emitter:
    def __init__(self):
        self.streams = {e: [] for e in ENGINES}
        self.waited = {e: {} for e in ENGINES}

    def wait(self, eng, evt):
        if evt is None:
            return
        if isinstance(evt, tuple):
            evt = {evt[0]: evt[1]}
        for sem, val in evt.items():
            if val <= 0 or self.waited[eng].get(sem, 0) >= val:
                continue
            self.waited[eng][sem] = val
            self.streams[eng].append(lambda e, s=sem, v=val: e.wait_ge(s.h, v))

    def op(self, eng, fn, inc=None, k=1):
        if inc is None:
            self.streams[eng].append(fn)
            return None
        inc.n += k
        tgt = inc.n
        self.streams[eng].append(lambda e, f=fn, s=inc, kk=k: f(e).then_inc(s.h, kk))
        return (inc, tgt)

    def inc_last(self, eng, sem):
        sem.n += 1
        tgt = sem.n
        last = self.streams[eng].pop()
        self.streams[eng].append(lambda e, f=last, s=sem: f(e).then_inc(s.h, 1))
        return (sem, tgt)

    def barrier(self, sems):
        tots = {s: s.n for s in sems}
        for eng in ENGINES:
            self.wait(eng, tots)


def merge(*evts):
    out = {}
    for e in evts:
        if e is None:
            continue
        if isinstance(e, tuple):
            e = {e[0]: e[1]}
        for s, v in e.items():
            if out.get(s, 0) < v:
                out[s] = v
    return out or None


def build(dbg=False):
    nc = bass.Bass("TRN2", target_bir_lowering=False)
    ikind = "ExternalOutput" if dbg else "Internal"

    x0T = nc.dram_tensor("x0T", [D, L], F32, kind="ExternalInput").ap()
    w1nH = nc.dram_tensor("w1n", [INTER, D], F32, kind="ExternalInput").ap()
    w2tH = nc.dram_tensor("w2t", [INTER, D], F32, kind="ExternalInput").ap()
    wqtH = nc.dram_tensor("wqt", [D, D], F32, kind="ExternalInput").ap()
    wktH = nc.dram_tensor("wkt", [D, D], F32, kind="ExternalInput").ap()
    wvtH = nc.dram_tensor("wvt", [D, D], F32, kind="ExternalInput").ap()
    wotH = nc.dram_tensor("wot", [D, D], F32, kind="ExternalInput").ap()
    orfTH = nc.dram_tensor("orfT", [DK, PM], F32, kind="ExternalInput").ap()
    bqcH = nc.dram_tensor("bqc", [128, 8], F32, kind="ExternalInput").ap()
    bkcH = nc.dram_tensor("bkc", [128, 8], F32, kind="ExternalInput").ap()
    bocH = nc.dram_tensor("boc", [128, 8], F32, kind="ExternalInput").ap()
    b1cH = nc.dram_tensor("b1c", [128, 32], F32, kind="ExternalInput").ap()
    b2cH = nc.dram_tensor("b2c", [128, 8], F32, kind="ExternalInput").ap()
    bvrowH = nc.dram_tensor("bvrow", [1, D], F32, kind="ExternalInput").ap()
    outT = nc.dram_tensor("outT", [D, L], F32, kind="ExternalOutput").ap()
    x1T = nc.dram_tensor("x1T", [D, L], BF16, kind=ikind).ap()
    x2T = nc.dram_tensor("x2T", [D, L], BF16, kind=ikind).ap()
    qTd = nc.dram_tensor("qTd", [D, L], BF16, kind=ikind).ap()

    ctx = ExitStack()
    sb = lambda name, shape, dt: ctx.enter_context(nc.sbuf_tensor(name, shape, dt))
    ps = lambda name: ctx.enter_context(nc.psum_tensor(name, [128, 512], F32))

    AR = sb("arB", [128, 73728], BF16)
    AFr = sb("arF", [128, 6816], F32)
    SQ = [sb("sq0", [128, 512], BF16), sb("sq1", [128, 512], BF16)]
    ET = [sb("et0", [128, 512], BF16), sb("et1", [128, 512], BF16)]
    SQA = sb("sqa", [128, 4096], BF16)
    BVB = sb("bvb", [128, 1024], BF16)
    MNT = sb("mnT", [128, 512], BF16)
    RBT = sb("rbT", [128, 512], BF16)
    SBT = sb("sbT", [128, 512], BF16)
    LNT = sb("lnT", [128, 512], BF16)
    LNT2 = sb("lnT2", [128, 512], BF16)
    ONES = sb("ones", [128, 128], BF16)
    ORF2 = sb("orf2", [128, 256], BF16)
    SEL = sb("sel", [128, 128], BF16)
    BVR = sb("bvr", [1, D], BF16)
    BQC = sb("bqc_s", [128, 8], F32)
    BKC = sb("bkc_s", [128, 8], F32)
    BOC = sb("boc_s", [128, 8], F32)
    KVROW = sb("kvrow", [1, 1040], BF16)
    KROW2 = sb("krow2", [1, 1024], BF16)
    B1C = sb("b1c_s", [128, 32], F32)
    B1B = sb("b1b_s", [128, 32], BF16)
    B2C = sb("b2c_s", [128, 8], F32)
    B2H = sb("b2h_s", [128, 8], F32)
    EPS5 = sb("eps5", [128, 1], F32)
    LNE = sb("lne", [128, 1], F32)

    PB = [ps(f"pb{i}") for i in range(8)]

    semT = Sem(ctx.enter_context(nc.semaphore("semT")))
    semV = Sem(ctx.enter_context(nc.semaphore("semV")))
    semS = Sem(ctx.enter_context(nc.semaphore("semS")))
    semW = Sem(ctx.enter_context(nc.semaphore("semW")))
    semB = Sem(ctx.enter_context(nc.semaphore("semB")))
    semX0 = Sem(ctx.enter_context(nc.semaphore("semX0")))
    semX1 = Sem(ctx.enter_context(nc.semaphore("semX1")))
    semH0 = Sem(ctx.enter_context(nc.semaphore("semH0")))
    semH1 = Sem(ctx.enter_context(nc.semaphore("semH1")))
    semQS = Sem(ctx.enter_context(nc.semaphore("semQS")))
    semQL = Sem(ctx.enter_context(nc.semaphore("semQL")))
    semXB = Sem(ctx.enter_context(nc.semaphore("semXB")))
    semX2S = Sem(ctx.enter_context(nc.semaphore("semX2S")))
    semSC = Sem(ctx.enter_context(nc.semaphore("semSC")))
    semSCg = Sem(ctx.enter_context(nc.semaphore("semSCg")))
    semG = Sem(ctx.enter_context(nc.semaphore("semG")))
    semX0g = Sem(ctx.enter_context(nc.semaphore("semX0g")))
    semX1g = Sem(ctx.enter_context(nc.semaphore("semX1g")))
    semWQ = [Sem(ctx.enter_context(nc.semaphore(f"semWQ{q}"))) for q in range(4)]
    ALLSEMS = [semT, semV, semS, semG, semW, semB, semX0, semX1, semH0, semH1,
               semQS, semQL, semXB, semX2S, semSC, semSCg, semX0g, semX1g] + semWQ

    em = Emitter()
    W, OP, TINC = em.wait, em.op, em.inc_last

    r_w1n, r_w2t, r_wc = 0, 32768, 65536
    r_wq, r_wk, r_wv = 0, 8192, 16384
    r_kc, r_vs, r_kphi = 24576, 28672, 32832
    r_kphi2 = 48272 + 4096
    r_kvbf, r_krep = 33856, 35936
    r_epsp, r_kvsum, r_ksr = 38016, 42112, 43152
    r_qc = 44176
    r_wo, r_e1, r_ap, r_q2 = 0, 8192, 10240, 14336
    f_kvP = [0, 2080]
    f_vsP = [4160, 5200]
    f_recip, f_nnk = 6240, 6752
    f_out = 0

    bank_free = {i: None for i in range(8)}

    def acquire(i):
        W("tensor", bank_free[i])

    qwev = []
    for q in range(4):
        OP("gpsimd", lambda e, q=q: e.dma_start(
            out=AR[:, r_w1n + q * 8192: r_w1n + (q + 1) * 8192]
                .rearrange("p (s d) -> p s d", s=8),
            in_=w1nH.rearrange("(s p) d -> p s d", p=128)[:, q * 8:(q + 1) * 8, :]),
            inc=semWQ[q], k=16)
        qwev.append(OP("gpsimd", lambda e, q=q: e.dma_start(
            out=AR[:, r_w2t + q * 8192: r_w2t + (q + 1) * 8192]
                .rearrange("p (s d) -> p s d", s=8),
            in_=w2tH.rearrange("(s p) d -> p s d", p=128)[:, q * 8:(q + 1) * 8, :]),
            inc=semWQ[q], k=16))
    OP("gpsimd", lambda e: e.dma_start(out=ORF2[0:64, :], in_=orfTH[:, :]), inc=semW, k=16)
    OP("gpsimd", lambda e: e.dma_start(out=ORF2[64:128, :], in_=orfTH[:, :]), inc=semW, k=16)
    ev_const = OP("gpsimd", lambda e: e.dma_start(out=BVR[:, :], in_=bvrowH[:, :]), inc=semW, k=16)
    ev_b = None
    for dst, src in ((BQC, bqcH), (BKC, bkcH), (BOC, bocH), (B1C, b1cH), (B2C, b2cH)):
        ev_b = OP("sync", lambda e, d=dst, s=src: e.dma_start(out=d[:, :], in_=s[:, :]), inc=semB, k=16)

    OP("vector", lambda e: e.memset(ONES[:, :], 1.0))
    v_sel0 = OP("vector", lambda e: e.memset(SEL[:, :], 0.0), inc=semV)
    W("vector", v_sel0)
    for i in range(8):
        OP("vector", lambda e, i=i: e.memset(SEL[0:64, i * 16 + 2 * i: i * 16 + 2 * i + 1], 1.0))
        OP("vector", lambda e, i=i: e.memset(SEL[64:128, i * 16 + 2 * i + 1: i * 16 + 2 * i + 2], 1.0))
    OP("vector", lambda e: e.memset(EPS5[:, :], 1e-5))
    OP("vector", lambda e: e.memset(LNE[:, :], LNEPS))
    W("vector", ev_b)
    ev_b1b = OP("vector", lambda e: e.tensor_copy(B1B[:, :], B1C[:, :]), inc=semV)
    OP("scalar", lambda e: e.add_instruction(mybir.InstLoadActFuncSet(
        name=nc.get_next_instruction_name(), act_func_set_id=6, ins=[], outs=[])))
    ev_wc = None
    for q in range(4):
        W("tensor", qwev[q])
        for ds in range(4):
            for half in range(2):
                bk = ds * 2 + half
                for j in range(q * 8, q * 8 + 8):
                    OP("tensor", lambda e, j=j, ds=ds, half=half, bk=bk: e.matmul(
                        PB[bk][:, :],
                        AR[:, r_w1n + j * 1024 + ds * 128: r_w1n + j * 1024 + (ds + 1) * 128],
                        AR[:, r_w2t + j * 1024 + half * 512: r_w2t + j * 1024 + (half + 1) * 512],
                        start=(j == 0), stop=(j == 31)))
    t_gA = TINC("tensor", semT)
    for ds in range(4):
        for half in range(2):
            bk = ds * 2 + half
            W("scalar", t_gA)
            ev_wc = OP("scalar", lambda e, ds=ds, half=half, bk=bk: e.activation(
                AR[:, r_wc + ds * 1024 + half * 512: r_wc + ds * 1024 + (half + 1) * 512],
                PB[bk][:, :], AF.Copy), inc=semS)
            bank_free[bk] = ev_wc
    for ds in range(4, NS):
        for half in range(2):
            bk = (ds - 4) * 2 + half
            acquire(bk)
            for j in range(32):
                OP("tensor", lambda e, j=j, ds=ds, half=half, bk=bk: e.matmul(
                    PB[bk][:, :],
                    AR[:, r_w1n + j * 1024 + ds * 128: r_w1n + j * 1024 + (ds + 1) * 128],
                    AR[:, r_w2t + j * 1024 + half * 512: r_w2t + j * 1024 + (half + 1) * 512],
                    start=(j == 0), stop=(j == 31)))
            t_g = TINC("tensor", semT)
            W("scalar", t_g)
            ev_wc = OP("scalar", lambda e, ds=ds, half=half, bk=bk: e.activation(
                AR[:, r_wc + ds * 1024 + half * 512: r_wc + ds * 1024 + (half + 1) * 512],
                PB[bk][:, :], AF.Copy), inc=semS)
            bank_free[bk] = ev_wc
    acquire(2)
    W("tensor", ev_b1b)
    for i in range(NS):
        for j in range(32):
            OP("tensor", lambda e, i=i, j=j: e.matmul(
                PB[2][:, i:i + 1],
                AR[:, r_w2t + j * 1024 + i * 128: r_w2t + j * 1024 + (i + 1) * 128],
                B1B[:, j:j + 1], start=(j == 0), stop=(j == 31)))
    t_g = TINC("tensor", semT)
    W("vector", t_g)
    ev_c2h = None
    for i in range(NS):
        ev_c2h = OP("vector", lambda e, i=i: e.tensor_scalar(
            B2H[:, i:i + 1], PB[2][:, i:i + 1], B2C[:, i:i + 1], 0.5,
            ALU.add, ALU.mult), inc=semV)
    bank_free[2] = ev_c2h
    ev_wc_all = t_g
    em.barrier(ALLSEMS)

    ln_srb = [None]
    ln_v6 = [None]

    def ln_emit(xbuf, x_evt, outbuf, ev_sqa_free=None, bankA=6, bankB=7):
        W("vector", merge(x_evt, ev_sqa_free, ln_v6[0]))
        v_sq = None
        for s in range(NS):
            xs = xbuf[:, s * CH:(s + 1) * CH]
            v_sq = OP("vector", lambda e, xs=xs, s=s: e.tensor_tensor(
                SQA[:, s * CH:(s + 1) * CH], xs, xs, ALU.mult),
                inc=semV if s == NS - 1 else None)
        acquire(bankA)
        acquire(bankB)
        W("tensor", x_evt)
        for s in range(NS):
            OP("tensor", lambda e, s=s, bA=bankA: e.matmul(
                PB[bA][:, :], ONES[:, :], xbuf[:, s * CH:(s + 1) * CH],
                start=(s == 0), stop=(s == NS - 1)))
        W("tensor", v_sq)
        for s in range(NS):
            OP("tensor", lambda e, s=s, bB=bankB: e.matmul(
                PB[bB][:, :], ONES[:, :], SQA[:, s * CH:(s + 1) * CH],
                start=(s == 0), stop=(s == NS - 1)))
        t_last = TINC("tensor", semT)
        W("vector", t_last)
        v1 = OP("vector", lambda e, bA=bankA: e.tensor_scalar(
            MNT[:, :], PB[bA][:, :], 1.0 / D, None, ALU.mult), inc=semV)
        v2 = OP("vector", lambda e, bB=bankB: e.tensor_scalar(
            SQ[0][:, :], PB[bB][:, :], 1.0 / D, None, ALU.mult), inc=semV)
        bank_free[bankA] = v1
        bank_free[bankB] = v2
        W("vector", v1)
        v3 = OP("vector", lambda e: e.tensor_tensor(
            SQ[1][:, :], MNT[:, :], MNT[:, :], ALU.mult), inc=semV)
        W("vector", v3)
        v4 = OP("vector", lambda e: e.tensor_tensor(
            LNT[:, :], SQ[0][:, :], SQ[1][:, :], ALU.subtract), inc=semV)
        W("scalar", merge(v4, ln_srb[0]))
        s1 = OP("scalar", lambda e: e.activation(
            LNT2[:, :], LNT[:, :], AF.Ln, bias=EPS5[:, 0:1]), inc=semS)
        W("scalar", s1)
        s_rb = OP("scalar", lambda e: e.activation(
            RBT[:, :], LNT2[:, :], AF.Exp, scale=-0.5), inc=semS)
        ln_srb[0] = s_rb
        W("vector", s_rb)
        v5 = OP("vector", lambda e: e.tensor_tensor(
            SBT[:, :], MNT[:, :], RBT[:, :], ALU.mult), inc=semV)
        v6 = None
        for s in range(NS):
            xs = xbuf[:, s * CH:(s + 1) * CH]
            v6 = OP("vector", lambda e, xs=xs, s=s: e.tensor_tensor(
                SQA[:, s * CH:(s + 1) * CH], xs, RBT[:, :], ALU.mult),
                inc=semV if s == NS - 1 else None)
        ln_v6[0] = v6
        W("vector", merge(v5, v6))
        v_ln = None
        for s in range(NS):
            ys = outbuf[:, s * CH:(s + 1) * CH]
            v_ln = OP("vector", lambda e, ys=ys, s=s: e.tensor_tensor(
                ys, SQA[:, s * CH:(s + 1) * CH], SBT[:, :], ALU.subtract),
                inc=semV if s == NS - 1 else None)
        return v_ln

    def ffn_pass(src, src_f32, dst, dst_f32, ev_w, prefetch=None, bufs=0,
                 src_gate=None):
        if bufs == 0:
            BX = [AR[:, 24576:28672], AR[:, 28672:32768], AR[:, 45056:49152]]
            BL = [AR[:, 32768:36864], AR[:, 36864:40960], AR[:, 49152:53248]]
            XST = AR[:, 40960:45056]
        else:
            BX = [AR[:, 45056:49152], AR[:, 49152:53248], AR[:, 53248:57344]]
            BL = [AR[:, 57344:61440], AR[:, 61440:65536], AR[:, 32768:36864]]
            XST = AR[:, 24576:28672]
        gpool = (0, 1, 2, 3, 4, 5) if bufs == 0 else (0, 1, 2, 3)
        nbx = len(BX)
        lq = "gpsimd"

        xsems = [semX0g, semX1g, semSCg]

        def emit_load(c):
            if src_gate is not None:
                W(lq, src_gate(c))
            return OP(lq, lambda e, c=c: e.dma_start(
                out=BX[c % nbx].rearrange("p (s t) -> p s t", s=8),
                in_=src.rearrange("(s p) t -> p s t", p=128)[:, :, c * CH:(c + 1) * CH]),
                inc=xsems[c % 3], k=16)

        def emit_store(c, half):
            if dst_f32:
                stg = AFr[:, f_out + half * 2048: f_out + (half + 1) * 2048]
            else:
                stg = XST[:, half * 2048:(half + 1) * 2048]
            return OP("sync", lambda e, c=c, half=half, stg=stg: e.dma_start(
                out=dst.rearrange("(s p) t -> p s t", p=128)
                    [:, half * 4:(half + 1) * 4, c * CH:(c + 1) * CH],
                in_=stg.rearrange("p (s t) -> p s t", s=4)),
                inc=(semH0 if half == 0 else semH1), k=16)

        lns, t_groups, ev_epi_by = {}, {}, {}
        ev_sthalf = [None, None]
        epi_par = [None, None]
        loads = {0: emit_load(0)}
        lns[0] = ln_emit(BX[0], loads[0], BL[0])
        loads[1] = emit_load(1)
        lns[1] = ln_emit(BX[1], loads[1], BL[1])
        for c in range(NCH):
            ev_ln = lns[c]
            ev_epi = None
            t_g = None
            for i in range(NS):
                bk = gpool[i % len(gpool)]
                acquire(bk)
                W("tensor", merge(ev_ln, ev_w))
                for s in range(NS):
                    OP("tensor", lambda e, s=s, i=i, bk=bk, c=c: e.matmul(
                        PB[bk][:, :],
                        AR[:, r_wc + s * 1024 + i * 128: r_wc + s * 1024 + (i + 1) * 128],
                        BL[c % 3][:, s * CH:(s + 1) * CH], start=(s == 0), stop=(s == NS - 1)))
                t_g = TINC("tensor", semT)
                if i == 0 and c + 2 < NCH:
                    W(lq, ev_epi_by.get(c - 1))
                    loads[c + 2] = emit_load(c + 2)
                if i == 0 and prefetch:
                    prefetch.pop(0)()
                if i == 4 and c + 2 < NCH:
                    W("vector", t_groups.get(c - 1))
                    lns[c + 2] = ln_emit(BX[(c + 2) % nbx], loads[c + 2], BL[(c + 2) % 3])
                W("scalar", merge(t_g, epi_par[i % 2]))
                v_ts = OP("scalar", lambda e, i=i, bk=bk: e.activation(
                    ET[i % 2][:, :], PB[bk][:, :], AF.Identity,
                    bias=B2H[:, i:i + 1], scale=0.5), inc=semS)
                W("vector", merge(v_ts, ev_sthalf[i // 4]))
                if dst_f32:
                    ev_epi = OP("vector", lambda e, i=i, c=c: e.tensor_tensor(
                        AFr[:, f_out + i * CH: f_out + (i + 1) * CH],
                        BX[c % nbx][:, i * CH:(i + 1) * CH], ET[i % 2][:, :], ALU.add), inc=semV)
                else:
                    ev_epi = OP("vector", lambda e, i=i, c=c: e.tensor_tensor(
                        XST[:, i * CH:(i + 1) * CH],
                        BX[c % nbx][:, i * CH:(i + 1) * CH], ET[i % 2][:, :], ALU.add), inc=semV)
                bank_free[bk] = ev_epi
                epi_par[i % 2] = ev_epi
                if i == 3:
                    W("sync", ev_epi)
                    ev_sthalf[0] = emit_store(c, 0)
            W("sync", ev_epi)
            ev_sthalf[1] = emit_store(c, 1)
            ev_epi_by[c] = ev_epi
            t_groups[c] = t_g
        return ev_sthalf[1]

    def _pf(wH, reg, half):
        def go():
            OP("gpsimd", lambda e: e.dma_start(
                out=AR[:, reg + half * 4096:reg + (half + 1) * 4096]
                    .rearrange("p (s d) -> p s d", s=4),
                in_=wH.rearrange("(s p) d -> p s d", p=128)[:, half * 4:(half + 1) * 4, :]),
                inc=semW, k=16)
        return go
    pf = [_pf(wqtH, r_wq, 0), _pf(wqtH, r_wq, 1), _pf(wktH, r_wk, 0),
          _pf(wktH, r_wk, 1), _pf(wvtH, r_wv, 0), _pf(wvtH, r_wv, 1)] + [lambda: None] * 8
    ev_wq = ev_wk = ev_wv = (semW, semW.n + 96)
    ffn_pass(x0T, True, x1T, False, merge(ev_wc, ev_c2h), prefetch=pf)
    em.barrier(ALLSEMS)

    OP("vector", lambda e: e.memset(
        AR[:, r_vs:r_vs + 4160].rearrange("p (a c) -> p a c", c=65)[:, :, 64:65], 1.0))
    OP("vector", lambda e: e.memset(AR[0:16, r_kvsum:r_kvsum + 1040], 0.0))
    OP("vector", lambda e: e.memset(AR[0:16, r_ksr:r_ksr + 1024], 0.0))
    OP("vector", lambda e: e.memset(AFr[:, f_kvP[0]:f_kvP[0] + 2080], 0.0))
    OP("vector", lambda e: e.memset(AFr[:, f_vsP[0]:f_vsP[0] + 1040], 0.0))

    def qkv_pass(wreg, wevt, bcol, dst, xlnb, ev_ln, ev_dst_free, sel_mode):
        ev_ev = None
        v_sq = None
        pool = (0, 1, 4, 5)
        for i in range(NS):
            bk = pool[i % 4]
            acquire(bk)
            W("tensor", merge(ev_ln, wevt))
            for s in range(NS):
                OP("tensor", lambda e, s=s, i=i, bk=bk, wreg=wreg, xlnb=xlnb: e.matmul(
                    PB[bk][:, :],
                    AR[:, wreg + s * 1024 + i * 128: wreg + s * 1024 + (i + 1) * 128],
                    xlnb[:, s * CH:(s + 1) * CH], start=(s == 0), stop=(s == NS - 1)))
            t_g = TINC("tensor", semT)
            W("scalar", merge(t_g, ev_dst_free))
            ev_ev = OP("scalar", lambda e, i=i, bk=bk, dst=dst, bcol=bcol: e.activation(
                AR[:, dst + i * CH: dst + (i + 1) * CH], PB[bk][:, :],
                AF.Identity, bias=bcol[:, i:i + 1]), inc=semS)
            bank_free[bk] = ev_ev
            W("vector", ev_ev)
            v_sq = OP("vector", lambda e, i=i, dst=dst: e.tensor_tensor(
                SQA[:, i * CH:(i + 1) * CH], AR[:, dst + i * CH: dst + (i + 1) * CH],
                AR[:, dst + i * CH: dst + (i + 1) * CH], ALU.mult),
                inc=semV if i == NS - 1 else None)
        acquire(2 if sel_mode == "q" else 3)
        W("tensor", v_sq)
        if sel_mode == "q":
            for i in range(NS):
                OP("tensor", lambda e, i=i: e.matmul(
                    PB[2][0:16, :], SEL[:, i * 16:(i + 1) * 16], SQA[:, i * CH:(i + 1) * CH],
                    start=(i == 0), stop=(i == NS - 1)))
        else:
            for tt in range(4):
                for i in range(NS):
                    OP("tensor", lambda e, i=i, tt=tt: e.matmul(
                        PB[3][:, tt * 16:(tt + 1) * 16],
                        SQA[:, i * CH + tt * 128: i * CH + (tt + 1) * 128],
                        SEL[:, i * 16:(i + 1) * 16],
                        start=(i == 0), stop=(i == NS - 1)))
        ev_sel = TINC("tensor", semT)
        return ev_ev, ev_sel

    acquire(0)
    acquire(1)
    W("tensor", ev_const)
    OP("tensor", lambda e: e.matmul(PB[0][:, 0:512], ONES[0:1, :], BVR[0:1, 0:512],
                                    start=True, stop=True))
    OP("tensor", lambda e: e.matmul(PB[1][:, 0:512], ONES[0:1, :], BVR[0:1, 512:1024],
                                    start=True, stop=True))
    t_bvb = TINC("tensor", semT)
    W("vector", t_bvb)
    OP("vector", lambda e: e.tensor_copy(BVB[:, 0:512], PB[0][:, :]))
    ev_bvb = OP("vector", lambda e: e.tensor_copy(BVB[:, 512:1024], PB[1][:, :]), inc=semV)
    bank_free[0] = ev_bvb
    bank_free[1] = ev_bvb

    aXL = [AR[:, 53392:57488], AR[:, 57488:61584]]
    aX0 = AR[:, 48272:52368]
    ev_q_stored = None
    ev_kc_free = None
    ev_vs_free = None
    ev_kv_add = None
    ev_vsum_add = None
    ev_epsp = None
    ev_kphi_used = [None, None]
    t_kv = None
    t_vlast = {}
    lns2 = {}
    loads2 = {}

    def emit_xload2(c):
        return OP("sync", lambda e, c=c: e.dma_start(
            out=aX0.rearrange("p (s t) -> p s t", s=8),
            in_=x1T.rearrange("(s p) t -> p s t", p=128)[:, :, c * CH:(c + 1) * CH]),
            inc=(semX0 if c % 2 == 0 else semX1), k=16)

    loads2[0] = emit_xload2(0)
    lns2[0] = ln_emit(aX0, loads2[0], aXL[0])
    for c in range(NCH):
        ev_ln = lns2[c]
        XL = aXL[c % 2]
        if c + 1 < NCH:
            W("sync", ev_ln)
            loads2[c + 1] = emit_xload2(c + 1)
        ev_qev, ev_selq = qkv_pass(r_wq, ev_wq, BQC, r_qc, XL, ev_ln, ev_q_stored, "q")
        W("scalar", ev_selq)
        ev_epsp = OP("scalar", lambda e, c=c: e.activation(
            AR[0:16, r_epsp + c * CH: r_epsp + (c + 1) * CH], PB[2][0:16, :],
            AF.Exp, bias=LNE[0:16, 0:1], scale=1.0 / 16.0), inc=semS)
        bank_free[2] = ev_epsp
        W("sync", ev_qev)
        ev_q_stored = OP("sync", lambda e, c=c: e.dma_start(
            out=qTd.rearrange("(s p) t -> p s t", p=128)[:, :, c * CH:(c + 1) * CH],
            in_=AR[:, r_qc:r_qc + 4096].rearrange("p (s t) -> p s t", s=8)), inc=semQS, k=16)
        ev_kev, ev_selk = qkv_pass(r_wk, ev_wk, BKC, r_kc, XL, ev_ln, ev_kc_free, "k")
        W("vector", ev_selk)
        ev_nnk = OP("vector", lambda e: e.tensor_scalar(
            AFr[:, f_nnk:f_nnk + 64], PB[3][:, 0:64], -1.0 / 16.0, None, ALU.mult), inc=semV)
        bank_free[3] = ev_nnk
        if c + 1 < NCH:
            W("vector", t_vlast.get(c - 1))
            lns2[c + 1] = ln_emit(aX0, loads2[c + 1], aXL[(c + 1) % 2],
                                  ev_sqa_free=ev_selk)
        ev_vev = None
        t_g = None
        vpool = (0, 1, 4, 5)
        for tt in range(4):
            for hf in range(2):
                bk = vpool[(tt * 2 + hf) % 4]
                acquire(bk)
                W("tensor", merge(ev_ln, ev_wv, ev_vs_free))
                for s in range(NS):
                    OP("tensor", lambda e, s=s, tt=tt, hf=hf, bk=bk, XL=XL: e.matmul(
                        PB[bk][:, :],
                        XL[:, s * CH + tt * 128: s * CH + (tt + 1) * 128],
                        AR[:, r_wv + s * 1024 + hf * 512: r_wv + s * 1024 + (hf + 1) * 512],
                        start=(s == 0), stop=(s == NS - 1)))
                t_g = TINC("tensor", semT)
                W("vector", merge(t_g, ev_bvb))
                ev_vev = OP("vector", lambda e, tt=tt, hf=hf, bk=bk: e.tensor_tensor(
                    AR[:, r_vs + tt * 1040 + hf * 520: r_vs + tt * 1040 + hf * 520 + 520]
                    .rearrange("p (h c) -> p h c", c=65)[:, :, 0:64],
                    PB[bk][:, :].rearrange("p (h c) -> p h c", c=64),
                    BVB[:, hf * 512:(hf + 1) * 512].rearrange("p (h c) -> p h c", c=64),
                    ALU.add), inc=semV)
                bank_free[bk] = ev_vev
        t_vlast[c] = t_g
        kv_old = f_kvP[c % 2]
        kv_new = f_kvP[(c + 1) % 2]
        units = [(h, tt) for h in range(H) for tt in range(4)]
        projd, expd = {}, {}

        def emit_proj(k):
            h, tt = units[k]
            prow0 = (h % 2) * 64
            pkb = 4 + (k % 4)
            acquire(pkb)
            W("tensor", merge(ev_kev, ev_const))
            OP("tensor", lambda e, h=h, tt=tt, pkb=pkb, prow0=prow0: e.matmul(
                PB[pkb][:, 0:256],
                AR[prow0:prow0 + 64,
                   r_kc + (h // 2) * 512 + tt * 128: r_kc + (h // 2) * 512 + (tt + 1) * 128],
                ORF2[prow0:prow0 + 64, :], start=True, stop=True))
            projd[k] = (TINC("tensor", semT), pkb)

        def emit_exp(k):
            h, tt = units[k]
            kp = r_kphi if h % 2 == 0 else r_kphi2
            t_g, pkb = projd.pop(k)
            W("scalar", merge(t_g, ev_nnk, ev_kphi_used[h % 2]))
            s_exp = OP("scalar", lambda e, h=h, tt=tt, pkb=pkb, kp=kp: e.activation(
                AR[:, kp + tt * 256: kp + (tt + 1) * 256], PB[pkb][:, 0:256],
                AF.Exp, bias=AFr[:, f_nnk + tt * 16 + h: f_nnk + tt * 16 + h + 1],
                scale=PSCALE), inc=semS)
            bank_free[pkb] = s_exp
            expd[k] = s_exp

        def emit_kv(k):
            nonlocal t_kv, ev_kv_add
            h, tt = units[k]
            kvbks = (0, 1) if h % 2 == 0 else (2, 3)
            kp = r_kphi if h % 2 == 0 else r_kphi2
            if tt == 0:
                acquire(kvbks[0])
                acquire(kvbks[1])
            W("tensor", merge(expd.pop(k), ev_vev, ev_kv_add))
            for mh in range(2):
                OP("tensor", lambda e, h=h, tt=tt, mh=mh, bk=kvbks[mh], kp=kp: e.matmul(
                    PB[bk][:, 0:65],
                    AR[:, kp + tt * 256 + mh * 128: kp + tt * 256 + (mh + 1) * 128],
                    AR[:, r_vs + tt * 1040 + h * 65: r_vs + tt * 1040 + h * 65 + 65],
                    start=(tt == 0), stop=(tt == 3)))
            if tt == 3:
                t_kv = TINC("tensor", semT)
                ev_kphi_used[h % 2] = t_kv
                W("vector", t_kv)
                for mh in range(2):
                    ev_kv_add = OP("vector", lambda e, h=h, mh=mh, bk=kvbks[mh],
                                   ko=kv_old, kn=kv_new: e.tensor_tensor(
                        AFr[:, kn + h * 130 + mh * 65: kn + h * 130 + (mh + 1) * 65],
                        AFr[:, ko + h * 130 + mh * 65: ko + h * 130 + (mh + 1) * 65],
                        PB[bk][:, 0:65], ALU.add), inc=semV)
                    bank_free[kvbks[mh]] = ev_kv_add

        for k in range(len(units)):
            emit_proj(k)
            if k >= 1:
                emit_exp(k - 1)
            if k >= 3:
                emit_kv(k - 3)
        emit_exp(len(units) - 1)
        emit_kv(len(units) - 3)
        emit_kv(len(units) - 2)
        emit_kv(len(units) - 1)
        vs_old = f_vsP[c % 2]
        vs_new = f_vsP[(c + 1) % 2]
        for vj, (off, n) in enumerate(((0, 512), (512, 512), (1024, 16))):
            vb = 6 + (vj % 2)
            acquire(vb)
            W("tensor", ev_vev)
            for tt in range(4):
                OP("tensor", lambda e, tt=tt, off=off, n=n, vb=vb: e.matmul(
                    PB[vb][:, 0:n], ONES[:, :],
                    AR[:, r_vs + tt * 1040 + off: r_vs + tt * 1040 + off + n],
                    start=(tt == 0), stop=(tt == 3)))
            t_g = TINC("tensor", semT)
            W("vector", t_g)
            ev_vsum_add = OP("vector", lambda e, off=off, n=n, vo=vs_old, vn=vs_new, vb=vb: e.tensor_tensor(
                AFr[:, vn + off: vn + off + n],
                AFr[:, vo + off: vo + off + n], PB[vb][:, 0:n], ALU.add), inc=semV)
            bank_free[vb] = ev_vsum_add
        ev_kc_free = t_kv
        ev_vs_free = t_kv
    W("gpsimd", ev_qev)
    ev_wo = OP("gpsimd", lambda e: e.dma_start(
        out=AR[:, r_wo:r_wo + 8192].rearrange("p (s d) -> p s d", s=8),
        in_=wotH.rearrange("(s p) d -> p s d", p=128)), inc=semW, k=16)
    W("vector", merge(ev_vsum_add, ev_kv_add))
    v_eps = OP("vector", lambda e: e.tensor_scalar(
        AFr[:, f_vsP[1]:f_vsP[1] + 1040], AFr[:, f_vsP[0]:f_vsP[0] + 1040],
        1e-6, None, ALU.mult), inc=semV)
    W("vector", v_eps)
    ev_kvbf = None
    for h in range(H):
        for mh in range(2):
            ev_kvbf = OP("vector", lambda e, h=h, mh=mh: e.tensor_tensor(
                AR[:, r_kvbf + h * 130 + mh * 65: r_kvbf + h * 130 + (mh + 1) * 65],
                AFr[:, f_kvP[0] + h * 130 + mh * 65: f_kvP[0] + h * 130 + (mh + 1) * 65],
                AFr[:, f_vsP[1] + h * 65: f_vsP[1] + (h + 1) * 65], ALU.add), inc=semV)
    ev_rows = None
    for h in range(H):
        bk = h % 2
        acquire(bk)
        W("tensor", ev_kvbf)
        for mh in range(2):
            OP("tensor", lambda e, h=h, mh=mh, bk=bk: e.matmul(
                PB[bk][0:1, 0:64], ONES[:, 0:1],
                AR[:, r_kvbf + h * 130 + mh * 65: r_kvbf + h * 130 + mh * 65 + 64],
                start=(mh == 0), stop=(mh == 1)))
        for mh in range(2):
            OP("tensor", lambda e, h=h, mh=mh, bk=bk: e.matmul(
                PB[bk][0:1, 496:497], ONES[:, 0:1],
                AR[:, r_kvbf + h * 130 + mh * 65 + 64: r_kvbf + h * 130 + mh * 65 + 65],
                start=(mh == 0), stop=(mh == 1)))
        t_g = TINC("tensor", semT)
        W("vector", t_g)
        OP("vector", lambda e, h=h, bk=bk: e.tensor_copy(
            KVROW[0:1, h * 65: h * 65 + 64], PB[bk][0:1, 0:64]))
        ev_kvr = OP("vector", lambda e, h=h, bk=bk: e.tensor_copy(
            KVROW[0:1, h * 65 + 64: h * 65 + 65], PB[bk][0:1, 496:497]), inc=semV)
        W("tensor", ev_kvr)
        OP("tensor", lambda e, h=h, bk=bk: e.matmul(
            PB[bk][0:1, 64:128], KVROW[0:1, h * 65 + 64: h * 65 + 65],
            ONES[0:1, 0:64], start=True, stop=True))
        t_g2 = TINC("tensor", semT)
        W("vector", t_g2)
        ev_rows = OP("vector", lambda e, h=h, bk=bk: e.tensor_copy(
            KROW2[0:1, h * 64: (h + 1) * 64], PB[bk][0:1, 64:128]), inc=semV)
        bank_free[bk] = ev_rows
    ev_scat = None
    for h in range(H):
        q = "sync" if h % 2 == 0 else "gpsimd"
        qs2 = semSC if h % 2 == 0 else semSCg
        W(q, ev_rows)
        OP(q, lambda e, h=h: e.dma_start(
            out=AR[h:h + 1, r_kvsum + h * 65: r_kvsum + (h + 1) * 65],
            in_=KVROW[0:1, h * 65: (h + 1) * 65]), inc=qs2, k=16)
        ev_scat = OP(q, lambda e, h=h: e.dma_start(
            out=AR[h:h + 1, r_ksr + h * 64: r_ksr + (h + 1) * 64],
            in_=KROW2[0:1, h * 64: (h + 1) * 64]), inc=qs2, k=16)
    ev_krep = None
    for h in range(H):
        for mh in range(2):
            ev_krep = OP("vector", lambda e, h=h, mh=mh: e.tensor_scalar(
                AR[:, r_krep + (2 * h + mh) * 64: r_krep + (2 * h + mh + 1) * 64],
                ONES[:, 0:64],
                AFr[:, f_kvP[0] + h * 130 + mh * 65 + 64: f_kvP[0] + h * 130 + mh * 65 + 65],
                float(4096 * 1e-6), ALU.mult, ALU.add), inc=semV)
    ev_p2a_t = (semT, semT.n)
    ev_scat_tot = {semSC: semSC.n, semSCg: semSCg.n}
    ev_qs_tot = (semQS, semQS.n)
    if dbg:
        d_kvbf = nc.dram_tensor("d_kvbf", [128, 2080], BF16, kind="ExternalOutput").ap()
        d_kvsb = nc.dram_tensor("d_kvsb", [128, 2080], F32, kind="ExternalOutput").ap()
        d_vsum = nc.dram_tensor("d_vsum", [128, 1040], F32, kind="ExternalOutput").ap()
        d_krep = nc.dram_tensor("d_krep", [128, 2048], BF16, kind="ExternalOutput").ap()
        d_epsp = nc.dram_tensor("d_epsp", [16, 4096], BF16, kind="ExternalOutput").ap()
        d_kvsum = nc.dram_tensor("d_kvsum", [16, 1040], BF16, kind="ExternalOutput").ap()
        d_ksr = nc.dram_tensor("d_ksr", [16, 1024], BF16, kind="ExternalOutput").ap()
        d_krow2 = nc.dram_tensor("d_krow2", [1, 1024], BF16, kind="ExternalOutput").ap()
        OP("sync", lambda e: e.dma_start(out=d_krow2[:, :], in_=KROW2[:, :]), inc=semSC, k=16)
        OP("sync", lambda e: e.dma_start(out=d_kvbf[:, :], in_=AR[:, r_kvbf:r_kvbf + 2080]), inc=semSC, k=16)
        OP("sync", lambda e: e.dma_start(out=d_kvsb[:, :], in_=AFr[:, f_kvP[0]:f_kvP[0] + 2080]), inc=semSC, k=16)
        OP("sync", lambda e: e.dma_start(out=d_vsum[:, :], in_=AFr[:, f_vsP[1]:f_vsP[1] + 1040]), inc=semSC, k=16)
        OP("sync", lambda e: e.dma_start(out=d_krep[:, :], in_=AR[:, r_krep:r_krep + 2048]), inc=semSC, k=16)
        OP("sync", lambda e: e.dma_start(out=d_epsp[:, :], in_=AR[0:16, r_epsp:r_epsp + 4096]), inc=semSC, k=16)
        OP("sync", lambda e: e.dma_start(out=d_kvsum[:, :], in_=AR[0:16, r_kvsum:r_kvsum + 1040]), inc=semSC, k=16)
        OP("sync", lambda e: e.dma_start(out=d_ksr[:, :], in_=AR[0:16, r_ksr:r_ksr + 1024]), inc=semSC, k=16)
        em.barrier(ALLSEMS)

    ev_ap_tread = None
    ev_x2st = None
    ev_epi2 = None
    qloads2, xloads2 = {}, {}

    q2buf = [AR[:, r_q2:r_q2 + 4096], AR[:, 18432:22528]]

    def emit_qload3(c):
        return OP("sync", lambda e, c=c: e.dma_start(
            out=q2buf[c % 2].rearrange("p (s t) -> p s t", s=8),
            in_=qTd.rearrange("(s p) t -> p s t", p=128)[:, :, c * CH:(c + 1) * CH]),
            inc=semQL, k=16)

    bX0 = AR[:, 24576:28672]
    bXST = AR[:, 28672:32768]

    def emit_xload3(c):
        return OP("sync", lambda e, c=c: e.dma_start(
            out=bX0.rearrange("p (s t) -> p s t", s=8),
            in_=x1T.rearrange("(s p) t -> p s t", p=128)[:, :, c * CH:(c + 1) * CH]),
            inc=semXB, k=16)

    tnd_by = {}
    W("sync", merge(ev_p2a_t, ev_qs_tot))
    qloads2[0] = emit_qload3(0)
    xloads2[0] = emit_xload3(0)
    for c in range(NCH):
        if c + 1 < NCH:
            W("sync", tnd_by.get(c - 1))
            qloads2[c + 1] = emit_qload3(c + 1)
        ev_qload = qloads2[c]
        ev_xload = xloads2[c]
        ev_e1_used = [None, None]
        t_nd = None
        ev_pair = None
        exps_by = {}
        pair_by = {}

        def emit_proj(h, ev_qload=ev_qload):
            prow0 = (h % 2) * 64
            s_exps = []
            for ms in range(2):
                pqb = (h % 2) * 6 + ms
                acquire(pqb)
                W("tensor", merge(ev_qload, ev_const))
                OP("tensor", lambda e, h=h, ms=ms, pqb=pqb, prow0=prow0, c=c: e.matmul(
                    PB[pqb][:, :], ORF2[prow0:prow0 + 64, ms * 128:(ms + 1) * 128],
                    q2buf[c % 2][prow0:prow0 + 64, (h // 2) * 512: (h // 2 + 1) * 512],
                    start=True, stop=True))
                t_g = TINC("tensor", semT)
                W("scalar", merge(t_g, ev_e1_used[h % 2]))
                s_exp = OP("scalar", lambda e, h=h, ms=ms, pqb=pqb: e.activation(
                    AR[:, r_e1 + (h % 2) * 1024 + ms * 512: r_e1 + (h % 2) * 1024 + (ms + 1) * 512],
                    PB[pqb][:, :], AF.Exp, scale=PSCALE), inc=semS)
                s_exps.append(s_exp)
                bank_free[pqb] = s_exp
            exps_by[h] = s_exps

        emit_proj(0)
        for h in range(H):
            if h + 1 < H:
                emit_proj(h + 1)
            prow0 = (h % 2) * 64
            numbk = 2 + ((h // 2) % 2)
            denbk = 4 + ((h // 2) % 2)
            s_exps = exps_by.pop(h)
            if h % 2 == 0:
                acquire(numbk)
                acquire(denbk)
            W("tensor", merge(s_exps[0], s_exps[1], ev_kvbf, ev_krep, ev_epsp,
                              ev_scat_tot))
            e1b = r_e1 + (h % 2) * 1024
            OP("tensor", lambda e, h=h, numbk=numbk, e1b=e1b, prow0=prow0: e.matmul(
                PB[numbk][prow0:prow0 + 64, :],
                AR[:, r_kvbf + h * 130: r_kvbf + h * 130 + 64],
                AR[:, e1b:e1b + 512], start=True, stop=False))
            OP("tensor", lambda e, h=h, numbk=numbk, e1b=e1b, prow0=prow0: e.matmul(
                PB[numbk][prow0:prow0 + 64, :],
                AR[:, r_kvbf + h * 130 + 65: r_kvbf + h * 130 + 65 + 64],
                AR[:, e1b + 512:e1b + 1024], start=False, stop=False))
            OP("tensor", lambda e, h=h, c=c, numbk=numbk, prow0=prow0: e.matmul(
                PB[numbk][prow0:prow0 + 64, :],
                AR[0:16, r_kvsum + h * 65: r_kvsum + h * 65 + 64],
                AR[0:16, r_epsp + c * CH: r_epsp + (c + 1) * CH],
                start=False, stop=True))
            OP("tensor", lambda e, h=h, denbk=denbk, e1b=e1b, prow0=prow0: e.matmul(
                PB[denbk][prow0:prow0 + 64, :],
                AR[:, r_krep + 2 * h * 64: r_krep + (2 * h + 1) * 64],
                AR[:, e1b:e1b + 512], start=True, stop=False))
            OP("tensor", lambda e, h=h, denbk=denbk, e1b=e1b, prow0=prow0: e.matmul(
                PB[denbk][prow0:prow0 + 64, :],
                AR[:, r_krep + (2 * h + 1) * 64: r_krep + (2 * h + 2) * 64],
                AR[:, e1b + 512:e1b + 1024], start=False, stop=False))
            OP("tensor", lambda e, h=h, c=c, denbk=denbk, prow0=prow0: e.matmul(
                PB[denbk][prow0:prow0 + 64, :],
                AR[0:16, r_ksr + h * 64: r_ksr + (h + 1) * 64],
                AR[0:16, r_epsp + c * CH: r_epsp + (c + 1) * CH],
                start=False, stop=True))
            t_nd = TINC("tensor", semT)
            ev_e1_used[h % 2] = t_nd
            if h % 2 == 1:
                W("vector", merge(t_nd, ev_ap_tread, ev_pair))
                v_rec = OP("vector", lambda e, denbk=denbk: e.reciprocal(
                    AFr[:, f_recip:f_recip + 512], PB[denbk][:, :]), inc=semV)
                W("vector", v_rec)
                ev_pair = OP("vector", lambda e, h=h, numbk=numbk: e.tensor_tensor(
                    AR[:, r_ap + (h // 2) * 512: r_ap + (h // 2 + 1) * 512],
                    PB[numbk][:, :], AFr[:, f_recip:f_recip + 512], ALU.mult), inc=semV)
                pair_by[h // 2] = ev_pair
                bank_free[numbk] = ev_pair
                bank_free[denbk] = ev_pair
        tnd_by[c] = t_nd
        t_po_last = None
        for i in range(NS):
            bk = 6 + (i % 2)
            acquire(bk)
            W("tensor", merge(ev_pair, ev_wo))
            for s in range(NS):
                OP("tensor", lambda e, s=s, i=i, bk=bk: e.matmul(
                    PB[bk][:, :],
                    AR[:, r_wo + s * 1024 + i * 128: r_wo + s * 1024 + (i + 1) * 128],
                    AR[:, r_ap + s * CH: r_ap + (s + 1) * CH], start=(s == 0), stop=(s == NS - 1)))
            t_g = TINC("tensor", semT)
            t_po_last = t_g
            W("scalar", merge(t_g, ev_epi2))
            tmp = RBT if i % 2 == 0 else SBT
            s_tmp = OP("scalar", lambda e, i=i, bk=bk, tmp=tmp: e.activation(
                tmp[:, :], PB[bk][:, :], AF.Identity, bias=BOC[:, i:i + 1]), inc=semS)
            W("vector", merge(s_tmp, ev_xload, ev_x2st))
            ev_epi2 = OP("vector", lambda e, i=i, tmp=tmp: e.tensor_tensor(
                bXST[:, i * CH:(i + 1) * CH], bX0[:, i * CH:(i + 1) * CH],
                tmp[:, :], ALU.add), inc=semV)
            bank_free[bk] = ev_epi2
        if c + 1 < NCH:
            W("sync", ev_epi2)
            xloads2[c + 1] = emit_xload3(c + 1)
        W("sync", ev_epi2)
        ev_x2st = OP("sync", lambda e, c=c: e.dma_start(
            out=x2T.rearrange("(s p) t -> p s t", p=128)[:, :, c * CH:(c + 1) * CH],
            in_=bXST.rearrange("p (s t) -> p s t", s=8)), inc=semX2S, k=16)
        ev_ap_tread = t_po_last

    em.wait("scalar", ev_epi2)
    em.wait("vector", ev_epi2)
    ffn_pass(x2T, False, outT, True, None, bufs=1,
             src_gate=lambda c: (semX2S, 16 * (c + 1)))

    em.wait("sync", {s: s.n for s in (semW, semB, semX0, semX1, semH0, semH1,
                                       semQS, semQL, semXB, semX2S, semSC, semSCg,
                                       semX0g, semX1g)})

    with nc.Block() as block:
        @block.tensor
        def _(e):
            for f in em.streams["tensor"]:
                f(e)

        @block.vector
        def _(e):
            for f in em.streams["vector"]:
                f(e)

        @block.scalar
        def _(e):
            for f in em.streams["scalar"]:
                f(e)

        @block.gpsimd
        def _(e):
            for f in em.streams["gpsimd"]:
                f(e)

        @block.sync
        def _(e):
            for f in em.streams["sync"]:
                f(e)

    ctx.close()
    return nc


def kernel(L0_attn_x, L0_attn_Wq, L0_attn_bq, L0_attn_Wk, L0_attn_bk,
           L0_attn_Wv, L0_attn_bv, L0_attn_Wo, L0_attn_bo, L0_attn_orf,
           L0_ffn_W1, L0_ffn_b1, L0_ffn_W2, L0_ffn_b2):
    f = np.float32
    x = np.asarray(L0_attn_x, f)
    Wq, Wk, Wv, Wo = (np.asarray(a, f) for a in (L0_attn_Wq, L0_attn_Wk, L0_attn_Wv, L0_attn_Wo))
    bq, bk, bv, bo = (np.asarray(a, f) for a in (L0_attn_bq, L0_attn_bk, L0_attn_bv, L0_attn_bo))
    orf = np.asarray(L0_attn_orf, f)
    W1, b1, W2, b2 = (np.asarray(a, f) for a in (L0_ffn_W1, L0_ffn_b1, L0_ffn_W2, L0_ffn_b2))

    col = lambda b: np.ascontiguousarray(b.reshape(-1, 128).T)
    wqT = np.ascontiguousarray(Wq.T)
    wkT = np.ascontiguousarray(Wk.T)
    wvT = np.ascontiguousarray(Wv.T)
    woT = np.ascontiguousarray(Wo.T)
    orfT = np.ascontiguousarray(orf.T)
    in_maps = []
    for c in range(8):
        in_maps.append({
            "x0T": np.ascontiguousarray(x[c].T),
            "w1n": np.ascontiguousarray(W1[c]),
            "w2t": np.ascontiguousarray(W2[c].T),
            "wqt": wqT, "wkt": wkT, "wvt": wvT, "wot": woT,
            "orfT": orfT,
            "bqc": col(bq), "bkc": col(bk), "boc": col(bo),
            "b1c": col(b1[c]), "b2c": col(b2[c]),
            "bvrow": np.ascontiguousarray(bv.reshape(1, -1)),
        })
    nc = build()
    res = run_bass_kernel_spmd(nc, in_maps, core_ids=list(range(8)))
    out = np.stack([np.ascontiguousarray(np.asarray(r["outT"]).T) for r in res.results])
    return out.astype(np.float32)


# version 33
# speedup vs baseline: 1.1824x; 1.1824x over previous
import numpy as np
import ml_dtypes
from contextlib import ExitStack

import concourse.bass as bass
import concourse.mybir as mybir
from concourse.bass_utils import run_bass_kernel_spmd

BF = ml_dtypes.bfloat16
F32 = mybir.dt.float32
BF16 = mybir.dt.bfloat16
AF = mybir.ActivationFunctionType
ALU = mybir.AluOpType

D = 1024
L = 4096
NS = 8
CH = 512
NCH = 8
H = 16
DK = 64
PM = 256
INTER = 4096
PSCALE = float(1.0 / (64.0 ** 0.25))
LNEPS = float(np.log(np.float32(1e-6)))

ENGINES = ("tensor", "vector", "scalar", "gpsimd", "sync")


class Sem:
    def __init__(self, handle):
        self.h = handle
        self.n = 0


class Emitter:
    def __init__(self):
        self.streams = {e: [] for e in ENGINES}
        self.waited = {e: {} for e in ENGINES}

    def wait(self, eng, evt):
        if evt is None:
            return
        if isinstance(evt, tuple):
            evt = {evt[0]: evt[1]}
        for sem, val in evt.items():
            if val <= 0 or self.waited[eng].get(sem, 0) >= val:
                continue
            self.waited[eng][sem] = val
            self.streams[eng].append(lambda e, s=sem, v=val: e.wait_ge(s.h, v))

    def op(self, eng, fn, inc=None, k=1):
        if inc is None:
            self.streams[eng].append(fn)
            return None
        inc.n += k
        tgt = inc.n
        self.streams[eng].append(lambda e, f=fn, s=inc, kk=k: f(e).then_inc(s.h, kk))
        return (inc, tgt)

    def inc_last(self, eng, sem):
        sem.n += 1
        tgt = sem.n
        last = self.streams[eng].pop()
        self.streams[eng].append(lambda e, f=last, s=sem: f(e).then_inc(s.h, 1))
        return (sem, tgt)

    def barrier(self, sems):
        tots = {s: s.n for s in sems}
        for eng in ENGINES:
            self.wait(eng, tots)


def merge(*evts):
    out = {}
    for e in evts:
        if e is None:
            continue
        if isinstance(e, tuple):
            e = {e[0]: e[1]}
        for s, v in e.items():
            if out.get(s, 0) < v:
                out[s] = v
    return out or None


def build(dbg=False):
    nc = bass.Bass("TRN2", target_bir_lowering=False)
    ikind = "ExternalOutput" if dbg else "Internal"

    x0T = nc.dram_tensor("x0T", [D, L], F32, kind="ExternalInput").ap()
    w1nH = nc.dram_tensor("w1n", [INTER, D], F32, kind="ExternalInput").ap()
    w2tH = nc.dram_tensor("w2t", [INTER, D], F32, kind="ExternalInput").ap()
    wqtH = nc.dram_tensor("wqt", [D, D], F32, kind="ExternalInput").ap()
    wktH = nc.dram_tensor("wkt", [D, D], F32, kind="ExternalInput").ap()
    wvtH = nc.dram_tensor("wvt", [D, D], F32, kind="ExternalInput").ap()
    wotH = nc.dram_tensor("wot", [D, D], F32, kind="ExternalInput").ap()
    orfTH = nc.dram_tensor("orfT", [DK, PM], F32, kind="ExternalInput").ap()
    bqcH = nc.dram_tensor("bqc", [128, 8], F32, kind="ExternalInput").ap()
    bkcH = nc.dram_tensor("bkc", [128, 8], F32, kind="ExternalInput").ap()
    bocH = nc.dram_tensor("boc", [128, 8], F32, kind="ExternalInput").ap()
    b1cH = nc.dram_tensor("b1c", [128, 32], F32, kind="ExternalInput").ap()
    b2cH = nc.dram_tensor("b2c", [128, 8], F32, kind="ExternalInput").ap()
    bvrowH = nc.dram_tensor("bvrow", [1, D], F32, kind="ExternalInput").ap()
    outT = nc.dram_tensor("outT", [D, L], F32, kind="ExternalOutput").ap()
    x1T = nc.dram_tensor("x1T", [D, L], BF16, kind=ikind).ap()
    x2T = nc.dram_tensor("x2T", [D, L], BF16, kind=ikind).ap()
    qTd = nc.dram_tensor("qTd", [D, L], BF16, kind=ikind).ap()

    ctx = ExitStack()
    sb = lambda name, shape, dt: ctx.enter_context(nc.sbuf_tensor(name, shape, dt))
    ps = lambda name: ctx.enter_context(nc.psum_tensor(name, [128, 512], F32))

    AR = sb("arB", [128, 73728], BF16)
    AFr = sb("arF", [128, 6816], F32)
    SQ = [sb("sq0", [128, 512], BF16), sb("sq1", [128, 512], BF16)]
    ET = [sb("et0", [128, 512], BF16), sb("et1", [128, 512], BF16)]
    SQA = sb("sqa", [128, 4096], BF16)
    XTR = sb("xtr", [128, 4096], BF16)
    BVB = sb("bvb", [128, 1024], BF16)
    MNT = sb("mnT", [128, 512], BF16)
    RBT = sb("rbT", [128, 512], BF16)
    SBT = sb("sbT", [128, 512], BF16)
    LNT = sb("lnT", [128, 512], BF16)
    LNT2 = sb("lnT2", [128, 512], BF16)
    ONES = sb("ones", [128, 128], BF16)
    ORF2 = sb("orf2", [128, 256], BF16)
    SEL = sb("sel", [128, 128], BF16)
    BVR = sb("bvr", [1, D], BF16)
    BQC = sb("bqc_s", [128, 8], F32)
    BKC = sb("bkc_s", [128, 8], F32)
    BOC = sb("boc_s", [128, 8], F32)
    KVROW = sb("kvrow", [1, 1040], BF16)
    KROW2 = sb("krow2", [1, 1024], BF16)
    B1C = sb("b1c_s", [128, 32], F32)
    B1B = sb("b1b_s", [128, 32], BF16)
    B2C = sb("b2c_s", [128, 8], F32)
    B2H = sb("b2h_s", [128, 8], F32)
    EPS5 = sb("eps5", [128, 1], F32)
    LNE = sb("lne", [128, 1], F32)

    PB = [ps(f"pb{i}") for i in range(8)]

    semT = Sem(ctx.enter_context(nc.semaphore("semT")))
    semV = Sem(ctx.enter_context(nc.semaphore("semV")))
    semS = Sem(ctx.enter_context(nc.semaphore("semS")))
    semW = Sem(ctx.enter_context(nc.semaphore("semW")))
    semB = Sem(ctx.enter_context(nc.semaphore("semB")))
    semX0 = Sem(ctx.enter_context(nc.semaphore("semX0")))
    semX1 = Sem(ctx.enter_context(nc.semaphore("semX1")))
    semH0 = Sem(ctx.enter_context(nc.semaphore("semH0")))
    semH1 = Sem(ctx.enter_context(nc.semaphore("semH1")))
    semQS = Sem(ctx.enter_context(nc.semaphore("semQS")))
    semQL = Sem(ctx.enter_context(nc.semaphore("semQL")))
    semXB = Sem(ctx.enter_context(nc.semaphore("semXB")))
    semX2S = Sem(ctx.enter_context(nc.semaphore("semX2S")))
    semSC = Sem(ctx.enter_context(nc.semaphore("semSC")))
    semSCg = Sem(ctx.enter_context(nc.semaphore("semSCg")))
    semG = Sem(ctx.enter_context(nc.semaphore("semG")))
    semX0g = Sem(ctx.enter_context(nc.semaphore("semX0g")))
    semX1g = Sem(ctx.enter_context(nc.semaphore("semX1g")))
    semWQ = [Sem(ctx.enter_context(nc.semaphore(f"semWQ{q}"))) for q in range(4)]
    ALLSEMS = [semT, semV, semS, semG, semW, semB, semX0, semX1, semH0, semH1,
               semQS, semQL, semXB, semX2S, semSC, semSCg, semX0g, semX1g] + semWQ

    em = Emitter()
    W, OP, TINC = em.wait, em.op, em.inc_last

    r_w1n, r_w2t, r_wc = 0, 32768, 65536
    r_wq, r_wk, r_wv = 0, 8192, 16384
    r_kc, r_vs, r_kphi = 24576, 28672, 32832
    r_kphi2 = 48272 + 4096
    r_kvbf, r_krep = 33856, 35936
    r_epsp, r_kvsum, r_ksr = 38016, 42112, 43152
    r_qc = 44176
    r_wo, r_e1, r_ap, r_q2 = 0, 8192, 10240, 14336
    f_kvP = [0, 2080]
    f_vsP = [4160, 5200]
    f_recip, f_nnk = 6240, 6752
    f_out = 0

    bank_free = {i: None for i in range(8)}

    def acquire(i):
        W("tensor", bank_free[i])

    qwev = []
    for q in range(4):
        OP("gpsimd", lambda e, q=q: e.dma_start(
            out=AR[:, r_w1n + q * 8192: r_w1n + (q + 1) * 8192]
                .rearrange("p (s d) -> p s d", s=8),
            in_=w1nH.rearrange("(s p) d -> p s d", p=128)[:, q * 8:(q + 1) * 8, :]),
            inc=semWQ[q], k=16)
        qwev.append(OP("gpsimd", lambda e, q=q: e.dma_start(
            out=AR[:, r_w2t + q * 8192: r_w2t + (q + 1) * 8192]
                .rearrange("p (s d) -> p s d", s=8),
            in_=w2tH.rearrange("(s p) d -> p s d", p=128)[:, q * 8:(q + 1) * 8, :]),
            inc=semWQ[q], k=16))
    OP("gpsimd", lambda e: e.dma_start(out=ORF2[0:64, :], in_=orfTH[:, :]), inc=semW, k=16)
    OP("gpsimd", lambda e: e.dma_start(out=ORF2[64:128, :], in_=orfTH[:, :]), inc=semW, k=16)
    ev_const = OP("gpsimd", lambda e: e.dma_start(out=BVR[:, :], in_=bvrowH[:, :]), inc=semW, k=16)
    ev_b = None
    for dst, src in ((BQC, bqcH), (BKC, bkcH), (BOC, bocH), (B1C, b1cH), (B2C, b2cH)):
        ev_b = OP("sync", lambda e, d=dst, s=src: e.dma_start(out=d[:, :], in_=s[:, :]), inc=semB, k=16)

    OP("vector", lambda e: e.memset(ONES[:, :], 1.0))
    v_sel0 = OP("vector", lambda e: e.memset(SEL[:, :], 0.0), inc=semV)
    W("vector", v_sel0)
    for i in range(8):
        OP("vector", lambda e, i=i: e.memset(SEL[0:64, i * 16 + 2 * i: i * 16 + 2 * i + 1], 1.0))
        OP("vector", lambda e, i=i: e.memset(SEL[64:128, i * 16 + 2 * i + 1: i * 16 + 2 * i + 2], 1.0))
    OP("vector", lambda e: e.memset(EPS5[:, :], 1e-5))
    OP("vector", lambda e: e.memset(LNE[:, :], LNEPS))
    W("vector", ev_b)
    ev_b1b = OP("vector", lambda e: e.tensor_copy(B1B[:, :], B1C[:, :]), inc=semV)
    OP("scalar", lambda e: e.add_instruction(mybir.InstLoadActFuncSet(
        name=nc.get_next_instruction_name(), act_func_set_id=6, ins=[], outs=[])))
    ev_wc = None
    for q in range(4):
        W("tensor", qwev[q])
        for ds in range(4):
            for half in range(2):
                bk = ds * 2 + half
                for j in range(q * 8, q * 8 + 8):
                    OP("tensor", lambda e, j=j, ds=ds, half=half, bk=bk: e.matmul(
                        PB[bk][:, :],
                        AR[:, r_w1n + j * 1024 + ds * 128: r_w1n + j * 1024 + (ds + 1) * 128],
                        AR[:, r_w2t + j * 1024 + half * 512: r_w2t + j * 1024 + (half + 1) * 512],
                        start=(j == 0), stop=(j == 31)))
    t_gA = TINC("tensor", semT)
    for ds in range(4):
        for half in range(2):
            bk = ds * 2 + half
            W("scalar", t_gA)
            ev_wc = OP("scalar", lambda e, ds=ds, half=half, bk=bk: e.activation(
                AR[:, r_wc + ds * 1024 + half * 512: r_wc + ds * 1024 + (half + 1) * 512],
                PB[bk][:, :], AF.Copy), inc=semS)
            bank_free[bk] = ev_wc
    for ds in range(4, NS):
        for half in range(2):
            bk = (ds - 4) * 2 + half
            acquire(bk)
            for j in range(32):
                OP("tensor", lambda e, j=j, ds=ds, half=half, bk=bk: e.matmul(
                    PB[bk][:, :],
                    AR[:, r_w1n + j * 1024 + ds * 128: r_w1n + j * 1024 + (ds + 1) * 128],
                    AR[:, r_w2t + j * 1024 + half * 512: r_w2t + j * 1024 + (half + 1) * 512],
                    start=(j == 0), stop=(j == 31)))
            t_g = TINC("tensor", semT)
            W("scalar", t_g)
            ev_wc = OP("scalar", lambda e, ds=ds, half=half, bk=bk: e.activation(
                AR[:, r_wc + ds * 1024 + half * 512: r_wc + ds * 1024 + (half + 1) * 512],
                PB[bk][:, :], AF.Copy), inc=semS)
            bank_free[bk] = ev_wc
    acquire(2)
    W("tensor", ev_b1b)
    for i in range(NS):
        for j in range(32):
            OP("tensor", lambda e, i=i, j=j: e.matmul(
                PB[2][:, i:i + 1],
                AR[:, r_w2t + j * 1024 + i * 128: r_w2t + j * 1024 + (i + 1) * 128],
                B1B[:, j:j + 1], start=(j == 0), stop=(j == 31)))
    t_g = TINC("tensor", semT)
    W("vector", t_g)
    ev_c2h = None
    for i in range(NS):
        ev_c2h = OP("vector", lambda e, i=i: e.tensor_scalar(
            B2H[:, i:i + 1], PB[2][:, i:i + 1], B2C[:, i:i + 1], 0.5,
            ALU.add, ALU.mult), inc=semV)
    bank_free[2] = ev_c2h
    ev_wc_all = t_g

    ln_srb = [None]
    ln_v6 = [None]

    def ln_emit(xbuf, x_evt, outbuf, ev_sqa_free=None, bankA=6, bankB=7):
        W("vector", merge(x_evt, ev_sqa_free, ln_v6[0]))
        v_sq = None
        for s in range(NS):
            xs = xbuf[:, s * CH:(s + 1) * CH]
            v_sq = OP("vector", lambda e, xs=xs, s=s: e.tensor_tensor(
                SQA[:, s * CH:(s + 1) * CH], xs, xs, ALU.mult),
                inc=semV if s == NS - 1 else None)
        acquire(bankA)
        acquire(bankB)
        W("tensor", x_evt)
        for s in range(NS):
            OP("tensor", lambda e, s=s, bA=bankA: e.matmul(
                PB[bA][:, :], ONES[:, :], xbuf[:, s * CH:(s + 1) * CH],
                start=(s == 0), stop=(s == NS - 1)))
        W("tensor", v_sq)
        for s in range(NS):
            OP("tensor", lambda e, s=s, bB=bankB: e.matmul(
                PB[bB][:, :], ONES[:, :], SQA[:, s * CH:(s + 1) * CH],
                start=(s == 0), stop=(s == NS - 1)))
        t_last = TINC("tensor", semT)
        W("vector", t_last)
        v1 = OP("vector", lambda e, bA=bankA: e.tensor_scalar(
            MNT[:, :], PB[bA][:, :], 1.0 / D, None, ALU.mult), inc=semV)
        v2 = OP("vector", lambda e, bB=bankB: e.tensor_scalar(
            SQ[0][:, :], PB[bB][:, :], 1.0 / D, None, ALU.mult), inc=semV)
        bank_free[bankA] = v1
        bank_free[bankB] = v2
        W("vector", v1)
        v3 = OP("vector", lambda e: e.tensor_tensor(
            SQ[1][:, :], MNT[:, :], MNT[:, :], ALU.mult), inc=semV)
        W("vector", v3)
        v4 = OP("vector", lambda e: e.tensor_tensor(
            LNT[:, :], SQ[0][:, :], SQ[1][:, :], ALU.subtract), inc=semV)
        W("scalar", merge(v4, ln_srb[0]))
        s1 = OP("scalar", lambda e: e.activation(
            LNT2[:, :], LNT[:, :], AF.Ln, bias=EPS5[:, 0:1]), inc=semS)
        W("scalar", s1)
        s_rb = OP("scalar", lambda e: e.activation(
            RBT[:, :], LNT2[:, :], AF.Exp, scale=-0.5), inc=semS)
        ln_srb[0] = s_rb
        W("vector", s_rb)
        v5 = OP("vector", lambda e: e.tensor_tensor(
            SBT[:, :], MNT[:, :], RBT[:, :], ALU.mult), inc=semV)
        v6 = None
        for s in range(NS):
            xs = xbuf[:, s * CH:(s + 1) * CH]
            v6 = OP("vector", lambda e, xs=xs, s=s: e.tensor_tensor(
                SQA[:, s * CH:(s + 1) * CH], xs, RBT[:, :], ALU.mult),
                inc=semV if s == NS - 1 else None)
        ln_v6[0] = v6
        W("vector", merge(v5, v6))
        v_ln = None
        for s in range(NS):
            ys = outbuf[:, s * CH:(s + 1) * CH]
            v_ln = OP("vector", lambda e, ys=ys, s=s: e.tensor_tensor(
                ys, SQA[:, s * CH:(s + 1) * CH], SBT[:, :], ALU.subtract),
                inc=semV if s == NS - 1 else None)
        return v_ln

    def ffn_pass(src, src_f32, dst, dst_f32, ev_w, prefetch=None, bufs=0,
                 src_gate=None, pre=None):
        if bufs == 0:
            BX = [AR[:, 24576:28672], AR[:, 28672:32768], AR[:, 45056:49152]]
            BL = [AR[:, 32768:36864], AR[:, 36864:40960], AR[:, 49152:53248]]
            XST = AR[:, 40960:45056]
        else:
            BX = [AR[:, 45056:49152], AR[:, 49152:53248], AR[:, 53248:57344]]
            BL = [AR[:, 57344:61440], AR[:, 61440:65536], AR[:, 32768:36864]]
            XST = AR[:, 24576:28672]
        if pre is not None:
            BX[0] = XTR[:, :]
        gpool = (0, 1, 2, 3, 4, 5) if bufs == 0 else (0, 1, 2, 3)
        nbx = len(BX)
        lq = "gpsimd"

        xsems = [semX0g, semX1g, semSCg]

        def emit_load(c):
            if src_gate is not None:
                W(lq, src_gate(c))
            return OP(lq, lambda e, c=c: e.dma_start(
                out=BX[c % nbx].rearrange("p (s t) -> p s t", s=8),
                in_=src.rearrange("(s p) t -> p s t", p=128)[:, :, c * CH:(c + 1) * CH]),
                inc=xsems[c % 3], k=16)

        def emit_store(c, half):
            if dst_f32:
                stg = AFr[:, f_out + half * 2048: f_out + (half + 1) * 2048]
            else:
                stg = XST[:, half * 2048:(half + 1) * 2048]
            return OP("sync", lambda e, c=c, half=half, stg=stg: e.dma_start(
                out=dst.rearrange("(s p) t -> p s t", p=128)
                    [:, half * 4:(half + 1) * 4, c * CH:(c + 1) * CH],
                in_=stg.rearrange("p (s t) -> p s t", s=4)),
                inc=(semH0 if half == 0 else semH1), k=16)

        lns, t_groups, ev_epi_by = {}, {}, {}
        ev_sthalf = [None, None]
        epi_par = [None, None]
        if pre is None:
            loads = {0: emit_load(0)}
            lns[0] = ln_emit(BX[0], loads[0], BL[0])
            loads[1] = emit_load(1)
            lns[1] = ln_emit(BX[1], loads[1], BL[1])
        else:
            loads = {0: pre[0]}
            lns[0] = pre[1]
            loads[1] = emit_load(1)
        for c in range(NCH):
            ev_ln = lns[c]
            ev_epi = None
            t_g = None
            for i in range(NS):
                bk = gpool[(c * NS + i) % len(gpool)]
                acquire(bk)
                W("tensor", merge(ev_ln, ev_w))
                for s in range(NS):
                    OP("tensor", lambda e, s=s, i=i, bk=bk, c=c: e.matmul(
                        PB[bk][:, :],
                        AR[:, r_wc + s * 1024 + i * 128: r_wc + s * 1024 + (i + 1) * 128],
                        BL[c % 3][:, s * CH:(s + 1) * CH], start=(s == 0), stop=(s == NS - 1)))
                t_g = TINC("tensor", semT)
                if i == 0 and c + 2 < NCH:
                    W(lq, ev_epi_by.get(c - 1))
                    loads[c + 2] = emit_load(c + 2)
                if i == 0 and prefetch:
                    prefetch.pop(0)()
                if i == 2 and c == 0 and pre is not None:
                    lns[1] = ln_emit(BX[1], loads[1], BL[1])
                if i == 4 and c + 2 < NCH:
                    W("vector", t_groups.get(c - 1))
                    lns[c + 2] = ln_emit(BX[(c + 2) % nbx], loads[c + 2], BL[(c + 2) % 3])
                W("scalar", merge(t_g, epi_par[i % 2]))
                v_ts = OP("scalar", lambda e, i=i, bk=bk: e.activation(
                    ET[i % 2][:, :], PB[bk][:, :], AF.Identity,
                    bias=B2H[:, i:i + 1], scale=0.5), inc=semS)
                W("vector", merge(v_ts, ev_sthalf[i // 4]))
                if dst_f32:
                    ev_epi = OP("vector", lambda e, i=i, c=c: e.tensor_tensor(
                        AFr[:, f_out + i * CH: f_out + (i + 1) * CH],
                        BX[c % nbx][:, i * CH:(i + 1) * CH], ET[i % 2][:, :], ALU.add), inc=semV)
                else:
                    ev_epi = OP("vector", lambda e, i=i, c=c: e.tensor_tensor(
                        XST[:, i * CH:(i + 1) * CH],
                        BX[c % nbx][:, i * CH:(i + 1) * CH], ET[i % 2][:, :], ALU.add), inc=semV)
                bank_free[bk] = ev_epi
                epi_par[i % 2] = ev_epi
                if i == 3:
                    W("sync", ev_epi)
                    ev_sthalf[0] = emit_store(c, 0)
            W("sync", ev_epi)
            ev_sthalf[1] = emit_store(c, 1)
            ev_epi_by[c] = ev_epi
            t_groups[c] = t_g
        return ev_sthalf[1]

    def _pf(wH, reg, half):
        def go():
            OP("gpsimd", lambda e: e.dma_start(
                out=AR[:, reg + half * 4096:reg + (half + 1) * 4096]
                    .rearrange("p (s d) -> p s d", s=4),
                in_=wH.rearrange("(s p) d -> p s d", p=128)[:, half * 4:(half + 1) * 4, :]),
                inc=semW, k=16)
        return go
    pf = [_pf(wqtH, r_wq, 0), _pf(wqtH, r_wq, 1), _pf(wktH, r_wk, 0),
          _pf(wktH, r_wk, 1), _pf(wvtH, r_wv, 0), _pf(wvtH, r_wv, 1)] + [lambda: None] * 8
    ev_wq = ev_wk = ev_wv = (semW, semW.n + 96)
    ld0 = OP("gpsimd", lambda e: e.dma_start(
        out=XTR[:, :].rearrange("p (s t) -> p s t", s=8),
        in_=x0T.rearrange("(s p) t -> p s t", p=128)[:, :, 0:CH]),
        inc=semX0g, k=16)
    ln0 = ln_emit(XTR[:, :], ld0, AR[:, 32768:36864])
    em.barrier(ALLSEMS)
    ffn_pass(x0T, True, x1T, False, merge(ev_wc, ev_c2h), prefetch=pf,
             pre=(ld0, ln0))
    em.barrier(ALLSEMS)

    OP("vector", lambda e: e.memset(
        AR[:, r_vs:r_vs + 4160].rearrange("p (a c) -> p a c", c=65)[:, :, 64:65], 1.0))
    OP("vector", lambda e: e.memset(AR[0:16, r_kvsum:r_kvsum + 1040], 0.0))
    OP("vector", lambda e: e.memset(AR[0:16, r_ksr:r_ksr + 1024], 0.0))
    OP("vector", lambda e: e.memset(AFr[:, f_kvP[0]:f_kvP[0] + 2080], 0.0))
    OP("vector", lambda e: e.memset(AFr[:, f_vsP[0]:f_vsP[0] + 1040], 0.0))

    def qkv_pass(wreg, wevt, bcol, dst, xlnb, ev_ln, ev_dst_free, sel_mode):
        ev_ev = None
        v_sq = None
        pool = (0, 1, 4, 5)
        for i in range(NS):
            bk = pool[i % 4]
            acquire(bk)
            W("tensor", merge(ev_ln, wevt))
            for s in range(NS):
                OP("tensor", lambda e, s=s, i=i, bk=bk, wreg=wreg, xlnb=xlnb: e.matmul(
                    PB[bk][:, :],
                    AR[:, wreg + s * 1024 + i * 128: wreg + s * 1024 + (i + 1) * 128],
                    xlnb[:, s * CH:(s + 1) * CH], start=(s == 0), stop=(s == NS - 1)))
            t_g = TINC("tensor", semT)
            W("scalar", merge(t_g, ev_dst_free))
            ev_ev = OP("scalar", lambda e, i=i, bk=bk, dst=dst, bcol=bcol: e.activation(
                AR[:, dst + i * CH: dst + (i + 1) * CH], PB[bk][:, :],
                AF.Identity, bias=bcol[:, i:i + 1]), inc=semS)
            bank_free[bk] = ev_ev
            W("vector", ev_ev)
            v_sq = OP("vector", lambda e, i=i, dst=dst: e.tensor_tensor(
                SQA[:, i * CH:(i + 1) * CH], AR[:, dst + i * CH: dst + (i + 1) * CH],
                AR[:, dst + i * CH: dst + (i + 1) * CH], ALU.mult),
                inc=semV if i == NS - 1 else None)
        acquire(2 if sel_mode == "q" else 3)
        W("tensor", v_sq)
        if sel_mode == "q":
            for i in range(NS):
                OP("tensor", lambda e, i=i: e.matmul(
                    PB[2][0:16, :], SEL[:, i * 16:(i + 1) * 16], SQA[:, i * CH:(i + 1) * CH],
                    start=(i == 0), stop=(i == NS - 1)))
        else:
            for tt in range(4):
                for i in range(NS):
                    OP("tensor", lambda e, i=i, tt=tt: e.matmul(
                        PB[3][:, tt * 16:(tt + 1) * 16],
                        SQA[:, i * CH + tt * 128: i * CH + (tt + 1) * 128],
                        SEL[:, i * 16:(i + 1) * 16],
                        start=(i == 0), stop=(i == NS - 1)))
        ev_sel = TINC("tensor", semT)
        return ev_ev, ev_sel

    acquire(0)
    acquire(1)
    W("tensor", ev_const)
    OP("tensor", lambda e: e.matmul(PB[0][:, 0:512], ONES[0:1, :], BVR[0:1, 0:512],
                                    start=True, stop=True))
    OP("tensor", lambda e: e.matmul(PB[1][:, 0:512], ONES[0:1, :], BVR[0:1, 512:1024],
                                    start=True, stop=True))
    t_bvb = TINC("tensor", semT)
    W("vector", t_bvb)
    OP("vector", lambda e: e.tensor_copy(BVB[:, 0:512], PB[0][:, :]))
    ev_bvb = OP("vector", lambda e: e.tensor_copy(BVB[:, 512:1024], PB[1][:, :]), inc=semV)
    bank_free[0] = ev_bvb
    bank_free[1] = ev_bvb

    aXL = [AR[:, 53392:57488], AR[:, 57488:61584]]
    aX0 = AR[:, 48272:52368]
    ev_q_stored = None
    ev_kc_free = None
    ev_vs_free = None
    ev_kv_add = None
    ev_vsum_add = None
    ev_epsp = None
    ev_kphi_used = [None, None]
    t_kv = None
    t_vlast = {}
    lns2 = {}
    loads2 = {}

    def emit_xload2(c):
        return OP("sync", lambda e, c=c: e.dma_start(
            out=aX0.rearrange("p (s t) -> p s t", s=8),
            in_=x1T.rearrange("(s p) t -> p s t", p=128)[:, :, c * CH:(c + 1) * CH]),
            inc=(semX0 if c % 2 == 0 else semX1), k=16)

    loads2[0] = emit_xload2(0)
    lns2[0] = ln_emit(aX0, loads2[0], aXL[0])
    for c in range(NCH):
        ev_ln = lns2[c]
        XL = aXL[c % 2]
        if c + 1 < NCH:
            W("sync", ev_ln)
            loads2[c + 1] = emit_xload2(c + 1)
        ev_qev, ev_selq = qkv_pass(r_wq, ev_wq, BQC, r_qc, XL, ev_ln, ev_q_stored, "q")
        W("scalar", ev_selq)
        ev_epsp = OP("scalar", lambda e, c=c: e.activation(
            AR[0:16, r_epsp + c * CH: r_epsp + (c + 1) * CH], PB[2][0:16, :],
            AF.Exp, bias=LNE[0:16, 0:1], scale=1.0 / 16.0), inc=semS)
        bank_free[2] = ev_epsp
        W("sync", ev_qev)
        ev_q_stored = OP("sync", lambda e, c=c: e.dma_start(
            out=qTd.rearrange("(s p) t -> p s t", p=128)[:, :, c * CH:(c + 1) * CH],
            in_=AR[:, r_qc:r_qc + 4096].rearrange("p (s t) -> p s t", s=8)), inc=semQS, k=16)
        ev_kev, ev_selk = qkv_pass(r_wk, ev_wk, BKC, r_kc, XL, ev_ln, ev_kc_free, "k")
        W("vector", ev_selk)
        ev_nnk = OP("vector", lambda e: e.tensor_scalar(
            AFr[:, f_nnk:f_nnk + 64], PB[3][:, 0:64], -1.0 / 16.0, None, ALU.mult), inc=semV)
        bank_free[3] = ev_nnk
        if c + 1 < NCH:
            W("vector", t_vlast.get(c - 1))
            lns2[c + 1] = ln_emit(aX0, loads2[c + 1], aXL[(c + 1) % 2],
                                  ev_sqa_free=ev_selk)
        ev_vev = None
        t_g = None
        vpool = (0, 1, 4, 5)
        for tt in range(4):
            for hf in range(2):
                bk = vpool[(tt * 2 + hf) % 4]
                acquire(bk)
                W("tensor", merge(ev_ln, ev_wv, ev_vs_free))
                for s in range(NS):
                    OP("tensor", lambda e, s=s, tt=tt, hf=hf, bk=bk, XL=XL: e.matmul(
                        PB[bk][:, :],
                        XL[:, s * CH + tt * 128: s * CH + (tt + 1) * 128],
                        AR[:, r_wv + s * 1024 + hf * 512: r_wv + s * 1024 + (hf + 1) * 512],
                        start=(s == 0), stop=(s == NS - 1)))
                t_g = TINC("tensor", semT)
                W("vector", merge(t_g, ev_bvb))
                ev_vev = OP("vector", lambda e, tt=tt, hf=hf, bk=bk: e.tensor_tensor(
                    AR[:, r_vs + tt * 1040 + hf * 520: r_vs + tt * 1040 + hf * 520 + 520]
                    .rearrange("p (h c) -> p h c", c=65)[:, :, 0:64],
                    PB[bk][:, :].rearrange("p (h c) -> p h c", c=64),
                    BVB[:, hf * 512:(hf + 1) * 512].rearrange("p (h c) -> p h c", c=64),
                    ALU.add), inc=semV)
                bank_free[bk] = ev_vev
        t_vlast[c] = t_g
        kv_old = f_kvP[c % 2]
        kv_new = f_kvP[(c + 1) % 2]
        units = [(h, tt) for h in range(H) for tt in range(4)]
        projd, expd = {}, {}

        def emit_proj(k):
            h, tt = units[k]
            prow0 = (h % 2) * 64
            pkb = 4 + (k % 4)
            acquire(pkb)
            W("tensor", merge(ev_kev, ev_const))
            OP("tensor", lambda e, h=h, tt=tt, pkb=pkb, prow0=prow0: e.matmul(
                PB[pkb][:, 0:256],
                AR[prow0:prow0 + 64,
                   r_kc + (h // 2) * 512 + tt * 128: r_kc + (h // 2) * 512 + (tt + 1) * 128],
                ORF2[prow0:prow0 + 64, :], start=True, stop=True))
            projd[k] = (TINC("tensor", semT), pkb)

        def emit_exp(k):
            h, tt = units[k]
            kp = r_kphi if h % 2 == 0 else r_kphi2
            t_g, pkb = projd.pop(k)
            W("scalar", merge(t_g, ev_nnk, ev_kphi_used[h % 2]))
            s_exp = OP("scalar", lambda e, h=h, tt=tt, pkb=pkb, kp=kp: e.activation(
                AR[:, kp + tt * 256: kp + (tt + 1) * 256], PB[pkb][:, 0:256],
                AF.Exp, bias=AFr[:, f_nnk + tt * 16 + h: f_nnk + tt * 16 + h + 1],
                scale=PSCALE), inc=semS)
            bank_free[pkb] = s_exp
            expd[k] = s_exp

        def emit_kv(k):
            nonlocal t_kv, ev_kv_add
            h, tt = units[k]
            kvbks = (0, 1) if h % 2 == 0 else (2, 3)
            kp = r_kphi if h % 2 == 0 else r_kphi2
            if tt == 0:
                acquire(kvbks[0])
                acquire(kvbks[1])
            W("tensor", merge(expd.pop(k), ev_vev, ev_kv_add))
            for mh in range(2):
                OP("tensor", lambda e, h=h, tt=tt, mh=mh, bk=kvbks[mh], kp=kp: e.matmul(
                    PB[bk][:, 0:65],
                    AR[:, kp + tt * 256 + mh * 128: kp + tt * 256 + (mh + 1) * 128],
                    AR[:, r_vs + tt * 1040 + h * 65: r_vs + tt * 1040 + h * 65 + 65],
                    start=(tt == 0), stop=(tt == 3)))
            if tt == 3:
                t_kv = TINC("tensor", semT)
                ev_kphi_used[h % 2] = t_kv
                W("vector", t_kv)
                for mh in range(2):
                    ev_kv_add = OP("vector", lambda e, h=h, mh=mh, bk=kvbks[mh],
                                   ko=kv_old, kn=kv_new: e.tensor_tensor(
                        AFr[:, kn + h * 130 + mh * 65: kn + h * 130 + (mh + 1) * 65],
                        AFr[:, ko + h * 130 + mh * 65: ko + h * 130 + (mh + 1) * 65],
                        PB[bk][:, 0:65], ALU.add), inc=semV)
                    bank_free[kvbks[mh]] = ev_kv_add

        for k in range(len(units)):
            emit_proj(k)
            if k >= 1:
                emit_exp(k - 1)
            if k >= 3:
                emit_kv(k - 3)
        emit_exp(len(units) - 1)
        emit_kv(len(units) - 3)
        emit_kv(len(units) - 2)
        emit_kv(len(units) - 1)
        vs_old = f_vsP[c % 2]
        vs_new = f_vsP[(c + 1) % 2]
        for vj, (off, n) in enumerate(((0, 512), (512, 512), (1024, 16))):
            vb = 6 + (vj % 2)
            acquire(vb)
            W("tensor", ev_vev)
            for tt in range(4):
                OP("tensor", lambda e, tt=tt, off=off, n=n, vb=vb: e.matmul(
                    PB[vb][:, 0:n], ONES[:, :],
                    AR[:, r_vs + tt * 1040 + off: r_vs + tt * 1040 + off + n],
                    start=(tt == 0), stop=(tt == 3)))
            t_g = TINC("tensor", semT)
            W("vector", t_g)
            ev_vsum_add = OP("vector", lambda e, off=off, n=n, vo=vs_old, vn=vs_new, vb=vb: e.tensor_tensor(
                AFr[:, vn + off: vn + off + n],
                AFr[:, vo + off: vo + off + n], PB[vb][:, 0:n], ALU.add), inc=semV)
            bank_free[vb] = ev_vsum_add
        ev_kc_free = t_kv
        ev_vs_free = t_kv
    W("gpsimd", ev_qev)
    ev_wo = OP("gpsimd", lambda e: e.dma_start(
        out=AR[:, r_wo:r_wo + 8192].rearrange("p (s d) -> p s d", s=8),
        in_=wotH.rearrange("(s p) d -> p s d", p=128)), inc=semW, k=16)
    W("vector", merge(ev_vsum_add, ev_kv_add))
    v_eps = OP("vector", lambda e: e.tensor_scalar(
        AFr[:, f_vsP[1]:f_vsP[1] + 1040], AFr[:, f_vsP[0]:f_vsP[0] + 1040],
        1e-6, None, ALU.mult), inc=semV)
    W("vector", v_eps)
    ev_kvbf = None
    for h in range(H):
        for mh in range(2):
            ev_kvbf = OP("vector", lambda e, h=h, mh=mh: e.tensor_tensor(
                AR[:, r_kvbf + h * 130 + mh * 65: r_kvbf + h * 130 + (mh + 1) * 65],
                AFr[:, f_kvP[0] + h * 130 + mh * 65: f_kvP[0] + h * 130 + (mh + 1) * 65],
                AFr[:, f_vsP[1] + h * 65: f_vsP[1] + (h + 1) * 65], ALU.add), inc=semV)
    ev_rows = None
    for h in range(H):
        bk = h % 2
        acquire(bk)
        W("tensor", ev_kvbf)
        for mh in range(2):
            OP("tensor", lambda e, h=h, mh=mh, bk=bk: e.matmul(
                PB[bk][0:1, 0:64], ONES[:, 0:1],
                AR[:, r_kvbf + h * 130 + mh * 65: r_kvbf + h * 130 + mh * 65 + 64],
                start=(mh == 0), stop=(mh == 1)))
        for mh in range(2):
            OP("tensor", lambda e, h=h, mh=mh, bk=bk: e.matmul(
                PB[bk][0:1, 496:497], ONES[:, 0:1],
                AR[:, r_kvbf + h * 130 + mh * 65 + 64: r_kvbf + h * 130 + mh * 65 + 65],
                start=(mh == 0), stop=(mh == 1)))
        t_g = TINC("tensor", semT)
        W("vector", t_g)
        OP("vector", lambda e, h=h, bk=bk: e.tensor_copy(
            KVROW[0:1, h * 65: h * 65 + 64], PB[bk][0:1, 0:64]))
        ev_kvr = OP("vector", lambda e, h=h, bk=bk: e.tensor_copy(
            KVROW[0:1, h * 65 + 64: h * 65 + 65], PB[bk][0:1, 496:497]), inc=semV)
        W("tensor", ev_kvr)
        OP("tensor", lambda e, h=h, bk=bk: e.matmul(
            PB[bk][0:1, 64:128], KVROW[0:1, h * 65 + 64: h * 65 + 65],
            ONES[0:1, 0:64], start=True, stop=True))
        t_g2 = TINC("tensor", semT)
        W("vector", t_g2)
        ev_rows = OP("vector", lambda e, h=h, bk=bk: e.tensor_copy(
            KROW2[0:1, h * 64: (h + 1) * 64], PB[bk][0:1, 64:128]), inc=semV)
        bank_free[bk] = ev_rows
    ev_scat = None
    for h in range(H):
        q = "sync" if h % 2 == 0 else "gpsimd"
        qs2 = semSC if h % 2 == 0 else semSCg
        W(q, ev_rows)
        OP(q, lambda e, h=h: e.dma_start(
            out=AR[h:h + 1, r_kvsum + h * 65: r_kvsum + (h + 1) * 65],
            in_=KVROW[0:1, h * 65: (h + 1) * 65]), inc=qs2, k=16)
        ev_scat = OP(q, lambda e, h=h: e.dma_start(
            out=AR[h:h + 1, r_ksr + h * 64: r_ksr + (h + 1) * 64],
            in_=KROW2[0:1, h * 64: (h + 1) * 64]), inc=qs2, k=16)
    ev_krep = None
    for h in range(H):
        for mh in range(2):
            ev_krep = OP("vector", lambda e, h=h, mh=mh: e.tensor_scalar(
                AR[:, r_krep + (2 * h + mh) * 64: r_krep + (2 * h + mh + 1) * 64],
                ONES[:, 0:64],
                AFr[:, f_kvP[0] + h * 130 + mh * 65 + 64: f_kvP[0] + h * 130 + mh * 65 + 65],
                float(4096 * 1e-6), ALU.mult, ALU.add), inc=semV)
    ev_p2a_t = (semT, semT.n)
    ev_scat_tot = {semSC: semSC.n, semSCg: semSCg.n}
    ev_qs_tot = (semQS, semQS.n)
    if dbg:
        d_kvbf = nc.dram_tensor("d_kvbf", [128, 2080], BF16, kind="ExternalOutput").ap()
        d_kvsb = nc.dram_tensor("d_kvsb", [128, 2080], F32, kind="ExternalOutput").ap()
        d_vsum = nc.dram_tensor("d_vsum", [128, 1040], F32, kind="ExternalOutput").ap()
        d_krep = nc.dram_tensor("d_krep", [128, 2048], BF16, kind="ExternalOutput").ap()
        d_epsp = nc.dram_tensor("d_epsp", [16, 4096], BF16, kind="ExternalOutput").ap()
        d_kvsum = nc.dram_tensor("d_kvsum", [16, 1040], BF16, kind="ExternalOutput").ap()
        d_ksr = nc.dram_tensor("d_ksr", [16, 1024], BF16, kind="ExternalOutput").ap()
        d_krow2 = nc.dram_tensor("d_krow2", [1, 1024], BF16, kind="ExternalOutput").ap()
        OP("sync", lambda e: e.dma_start(out=d_krow2[:, :], in_=KROW2[:, :]), inc=semSC, k=16)
        OP("sync", lambda e: e.dma_start(out=d_kvbf[:, :], in_=AR[:, r_kvbf:r_kvbf + 2080]), inc=semSC, k=16)
        OP("sync", lambda e: e.dma_start(out=d_kvsb[:, :], in_=AFr[:, f_kvP[0]:f_kvP[0] + 2080]), inc=semSC, k=16)
        OP("sync", lambda e: e.dma_start(out=d_vsum[:, :], in_=AFr[:, f_vsP[1]:f_vsP[1] + 1040]), inc=semSC, k=16)
        OP("sync", lambda e: e.dma_start(out=d_krep[:, :], in_=AR[:, r_krep:r_krep + 2048]), inc=semSC, k=16)
        OP("sync", lambda e: e.dma_start(out=d_epsp[:, :], in_=AR[0:16, r_epsp:r_epsp + 4096]), inc=semSC, k=16)
        OP("sync", lambda e: e.dma_start(out=d_kvsum[:, :], in_=AR[0:16, r_kvsum:r_kvsum + 1040]), inc=semSC, k=16)
        OP("sync", lambda e: e.dma_start(out=d_ksr[:, :], in_=AR[0:16, r_ksr:r_ksr + 1024]), inc=semSC, k=16)
        em.barrier(ALLSEMS)

    ev_ap_tread = None
    ev_x2st = None
    ev_epi2 = None
    qloads2, xloads2 = {}, {}

    q2buf = [AR[:, r_q2:r_q2 + 4096], AR[:, 18432:22528]]

    def emit_qload3(c):
        return OP("sync", lambda e, c=c: e.dma_start(
            out=q2buf[c % 2].rearrange("p (s t) -> p s t", s=8),
            in_=qTd.rearrange("(s p) t -> p s t", p=128)[:, :, c * CH:(c + 1) * CH]),
            inc=semQL, k=16)

    bX0 = AR[:, 24576:28672]
    bXST = AR[:, 28672:32768]

    def emit_xload3(c):
        return OP("sync", lambda e, c=c: e.dma_start(
            out=bX0.rearrange("p (s t) -> p s t", s=8),
            in_=x1T.rearrange("(s p) t -> p s t", p=128)[:, :, c * CH:(c + 1) * CH]),
            inc=semXB, k=16)

    tnd_by = {}
    W("sync", merge(ev_p2a_t, ev_qs_tot))
    qloads2[0] = emit_qload3(0)
    xloads2[0] = emit_xload3(0)
    for c in range(NCH):
        if c + 1 < NCH:
            W("sync", tnd_by.get(c - 1))
            qloads2[c + 1] = emit_qload3(c + 1)
        ev_qload = qloads2[c]
        ev_xload = xloads2[c]
        ev_e1_used = [None, None]
        t_nd = None
        ev_pair = None
        exps_by = {}
        pair_by = {}

        def emit_proj(h, ev_qload=ev_qload):
            prow0 = (h % 2) * 64
            s_exps = []
            for ms in range(2):
                pqb = (h % 2) * 6 + ms
                acquire(pqb)
                W("tensor", merge(ev_qload, ev_const))
                OP("tensor", lambda e, h=h, ms=ms, pqb=pqb, prow0=prow0, c=c: e.matmul(
                    PB[pqb][:, :], ORF2[prow0:prow0 + 64, ms * 128:(ms + 1) * 128],
                    q2buf[c % 2][prow0:prow0 + 64, (h // 2) * 512: (h // 2 + 1) * 512],
                    start=True, stop=True))
                t_g = TINC("tensor", semT)
                W("scalar", merge(t_g, ev_e1_used[h % 2]))
                s_exp = OP("scalar", lambda e, h=h, ms=ms, pqb=pqb: e.activation(
                    AR[:, r_e1 + (h % 2) * 1024 + ms * 512: r_e1 + (h % 2) * 1024 + (ms + 1) * 512],
                    PB[pqb][:, :], AF.Exp, scale=PSCALE), inc=semS)
                s_exps.append(s_exp)
                bank_free[pqb] = s_exp
            exps_by[h] = s_exps

        emit_proj(0)
        for h in range(H):
            if h + 1 < H:
                emit_proj(h + 1)
            prow0 = (h % 2) * 64
            numbk = 2 + ((h // 2) % 2)
            denbk = 4 + ((h // 2) % 2)
            s_exps = exps_by.pop(h)
            if h % 2 == 0:
                acquire(numbk)
                acquire(denbk)
            W("tensor", merge(s_exps[0], s_exps[1], ev_kvbf, ev_krep, ev_epsp,
                              ev_scat_tot))
            e1b = r_e1 + (h % 2) * 1024
            OP("tensor", lambda e, h=h, numbk=numbk, e1b=e1b, prow0=prow0: e.matmul(
                PB[numbk][prow0:prow0 + 64, :],
                AR[:, r_kvbf + h * 130: r_kvbf + h * 130 + 64],
                AR[:, e1b:e1b + 512], start=True, stop=False))
            OP("tensor", lambda e, h=h, numbk=numbk, e1b=e1b, prow0=prow0: e.matmul(
                PB[numbk][prow0:prow0 + 64, :],
                AR[:, r_kvbf + h * 130 + 65: r_kvbf + h * 130 + 65 + 64],
                AR[:, e1b + 512:e1b + 1024], start=False, stop=False))
            OP("tensor", lambda e, h=h, c=c, numbk=numbk, prow0=prow0: e.matmul(
                PB[numbk][prow0:prow0 + 64, :],
                AR[0:16, r_kvsum + h * 65: r_kvsum + h * 65 + 64],
                AR[0:16, r_epsp + c * CH: r_epsp + (c + 1) * CH],
                start=False, stop=True))
            OP("tensor", lambda e, h=h, denbk=denbk, e1b=e1b, prow0=prow0: e.matmul(
                PB[denbk][prow0:prow0 + 64, :],
                AR[:, r_krep + 2 * h * 64: r_krep + (2 * h + 1) * 64],
                AR[:, e1b:e1b + 512], start=True, stop=False))
            OP("tensor", lambda e, h=h, denbk=denbk, e1b=e1b, prow0=prow0: e.matmul(
                PB[denbk][prow0:prow0 + 64, :],
                AR[:, r_krep + (2 * h + 1) * 64: r_krep + (2 * h + 2) * 64],
                AR[:, e1b + 512:e1b + 1024], start=False, stop=False))
            OP("tensor", lambda e, h=h, c=c, denbk=denbk, prow0=prow0: e.matmul(
                PB[denbk][prow0:prow0 + 64, :],
                AR[0:16, r_ksr + h * 64: r_ksr + (h + 1) * 64],
                AR[0:16, r_epsp + c * CH: r_epsp + (c + 1) * CH],
                start=False, stop=True))
            t_nd = TINC("tensor", semT)
            ev_e1_used[h % 2] = t_nd
            if h % 2 == 1:
                W("vector", merge(t_nd, ev_ap_tread, ev_pair))
                v_rec = OP("vector", lambda e, denbk=denbk: e.reciprocal(
                    AFr[:, f_recip:f_recip + 512], PB[denbk][:, :]), inc=semV)
                W("vector", v_rec)
                ev_pair = OP("vector", lambda e, h=h, numbk=numbk: e.tensor_tensor(
                    AR[:, r_ap + (h // 2) * 512: r_ap + (h // 2 + 1) * 512],
                    PB[numbk][:, :], AFr[:, f_recip:f_recip + 512], ALU.mult), inc=semV)
                pair_by[h // 2] = ev_pair
                bank_free[numbk] = ev_pair
                bank_free[denbk] = ev_pair
        tnd_by[c] = t_nd
        t_po_last = None
        for i in range(NS):
            bk = 6 + (i % 2)
            acquire(bk)
            W("tensor", merge(ev_pair, ev_wo))
            for s in range(NS):
                OP("tensor", lambda e, s=s, i=i, bk=bk: e.matmul(
                    PB[bk][:, :],
                    AR[:, r_wo + s * 1024 + i * 128: r_wo + s * 1024 + (i + 1) * 128],
                    AR[:, r_ap + s * CH: r_ap + (s + 1) * CH], start=(s == 0), stop=(s == NS - 1)))
            t_g = TINC("tensor", semT)
            t_po_last = t_g
            W("scalar", merge(t_g, ev_epi2))
            tmp = RBT if i % 2 == 0 else SBT
            s_tmp = OP("scalar", lambda e, i=i, bk=bk, tmp=tmp: e.activation(
                tmp[:, :], PB[bk][:, :], AF.Identity, bias=BOC[:, i:i + 1]), inc=semS)
            W("vector", merge(s_tmp, ev_xload, ev_x2st))
            ev_epi2 = OP("vector", lambda e, i=i, tmp=tmp: e.tensor_tensor(
                bXST[:, i * CH:(i + 1) * CH], bX0[:, i * CH:(i + 1) * CH],
                tmp[:, :], ALU.add), inc=semV)
            bank_free[bk] = ev_epi2
        if c + 1 < NCH:
            W("sync", ev_epi2)
            xloads2[c + 1] = emit_xload3(c + 1)
        W("sync", ev_epi2)
        ev_x2st = OP("sync", lambda e, c=c: e.dma_start(
            out=x2T.rearrange("(s p) t -> p s t", p=128)[:, :, c * CH:(c + 1) * CH],
            in_=bXST.rearrange("p (s t) -> p s t", s=8)), inc=semX2S, k=16)
        ev_ap_tread = t_po_last

    em.wait("scalar", ev_epi2)
    em.wait("vector", ev_epi2)
    ffn_pass(x2T, False, outT, True, None, bufs=1,
             src_gate=lambda c: (semX2S, 16 * (c + 1)))

    em.wait("sync", {s: s.n for s in (semW, semB, semX0, semX1, semH0, semH1,
                                       semQS, semQL, semXB, semX2S, semSC, semSCg,
                                       semX0g, semX1g)})

    with nc.Block() as block:
        @block.tensor
        def _(e):
            for f in em.streams["tensor"]:
                f(e)

        @block.vector
        def _(e):
            for f in em.streams["vector"]:
                f(e)

        @block.scalar
        def _(e):
            for f in em.streams["scalar"]:
                f(e)

        @block.gpsimd
        def _(e):
            for f in em.streams["gpsimd"]:
                f(e)

        @block.sync
        def _(e):
            for f in em.streams["sync"]:
                f(e)

    ctx.close()
    return nc


def kernel(L0_attn_x, L0_attn_Wq, L0_attn_bq, L0_attn_Wk, L0_attn_bk,
           L0_attn_Wv, L0_attn_bv, L0_attn_Wo, L0_attn_bo, L0_attn_orf,
           L0_ffn_W1, L0_ffn_b1, L0_ffn_W2, L0_ffn_b2):
    f = np.float32
    x = np.asarray(L0_attn_x, f)
    Wq, Wk, Wv, Wo = (np.asarray(a, f) for a in (L0_attn_Wq, L0_attn_Wk, L0_attn_Wv, L0_attn_Wo))
    bq, bk, bv, bo = (np.asarray(a, f) for a in (L0_attn_bq, L0_attn_bk, L0_attn_bv, L0_attn_bo))
    orf = np.asarray(L0_attn_orf, f)
    W1, b1, W2, b2 = (np.asarray(a, f) for a in (L0_ffn_W1, L0_ffn_b1, L0_ffn_W2, L0_ffn_b2))

    col = lambda b: np.ascontiguousarray(b.reshape(-1, 128).T)
    wqT = np.ascontiguousarray(Wq.T)
    wkT = np.ascontiguousarray(Wk.T)
    wvT = np.ascontiguousarray(Wv.T)
    woT = np.ascontiguousarray(Wo.T)
    orfT = np.ascontiguousarray(orf.T)
    in_maps = []
    for c in range(8):
        in_maps.append({
            "x0T": np.ascontiguousarray(x[c].T),
            "w1n": np.ascontiguousarray(W1[c]),
            "w2t": np.ascontiguousarray(W2[c].T),
            "wqt": wqT, "wkt": wkT, "wvt": wvT, "wot": woT,
            "orfT": orfT,
            "bqc": col(bq), "bkc": col(bk), "boc": col(bo),
            "b1c": col(b1[c]), "b2c": col(b2[c]),
            "bvrow": np.ascontiguousarray(bv.reshape(1, -1)),
        })
    nc = build()
    res = run_bass_kernel_spmd(nc, in_maps, core_ids=list(range(8)))
    out = np.stack([np.ascontiguousarray(np.asarray(r["outT"]).T) for r in res.results])
    return out.astype(np.float32)


# version 34
# speedup vs baseline: 1.1952x; 1.0108x over previous
import numpy as np
import ml_dtypes
from contextlib import ExitStack

import concourse.bass as bass
import concourse.mybir as mybir
from concourse.bass_utils import run_bass_kernel_spmd

BF = ml_dtypes.bfloat16
F32 = mybir.dt.float32
BF16 = mybir.dt.bfloat16
AF = mybir.ActivationFunctionType
ALU = mybir.AluOpType

D = 1024
L = 4096
NS = 8
CH = 512
NCH = 8
H = 16
DK = 64
PM = 256
INTER = 4096
PSCALE = float(1.0 / (64.0 ** 0.25))
LNEPS = float(np.log(np.float32(1e-6)))

ENGINES = ("tensor", "vector", "scalar", "gpsimd", "sync")


class Sem:
    def __init__(self, handle):
        self.h = handle
        self.n = 0


class Emitter:
    def __init__(self):
        self.streams = {e: [] for e in ENGINES}
        self.waited = {e: {} for e in ENGINES}

    def wait(self, eng, evt):
        if evt is None:
            return
        if isinstance(evt, tuple):
            evt = {evt[0]: evt[1]}
        for sem, val in evt.items():
            if val <= 0 or self.waited[eng].get(sem, 0) >= val:
                continue
            self.waited[eng][sem] = val
            self.streams[eng].append(lambda e, s=sem, v=val: e.wait_ge(s.h, v))

    def op(self, eng, fn, inc=None, k=1):
        if inc is None:
            self.streams[eng].append(fn)
            return None
        inc.n += k
        tgt = inc.n
        self.streams[eng].append(lambda e, f=fn, s=inc, kk=k: f(e).then_inc(s.h, kk))
        return (inc, tgt)

    def inc_last(self, eng, sem):
        sem.n += 1
        tgt = sem.n
        last = self.streams[eng].pop()
        self.streams[eng].append(lambda e, f=last, s=sem: f(e).then_inc(s.h, 1))
        return (sem, tgt)

    def barrier(self, sems):
        tots = {s: s.n for s in sems}
        for eng in ENGINES:
            self.wait(eng, tots)


def merge(*evts):
    out = {}
    for e in evts:
        if e is None:
            continue
        if isinstance(e, tuple):
            e = {e[0]: e[1]}
        for s, v in e.items():
            if out.get(s, 0) < v:
                out[s] = v
    return out or None


def build(dbg=False):
    nc = bass.Bass("TRN2", target_bir_lowering=False)
    ikind = "ExternalOutput" if dbg else "Internal"

    x0T = nc.dram_tensor("x0T", [D, L], F32, kind="ExternalInput").ap()
    w1nH = nc.dram_tensor("w1n", [INTER, D], F32, kind="ExternalInput").ap()
    w2tH = nc.dram_tensor("w2t", [INTER, D], F32, kind="ExternalInput").ap()
    wqtH = nc.dram_tensor("wqt", [D, D], F32, kind="ExternalInput").ap()
    wktH = nc.dram_tensor("wkt", [D, D], F32, kind="ExternalInput").ap()
    wvtH = nc.dram_tensor("wvt", [D, D], F32, kind="ExternalInput").ap()
    wotH = nc.dram_tensor("wot", [D, D], F32, kind="ExternalInput").ap()
    orfTH = nc.dram_tensor("orfT", [DK, PM], F32, kind="ExternalInput").ap()
    bqcH = nc.dram_tensor("bqc", [128, 8], F32, kind="ExternalInput").ap()
    bkcH = nc.dram_tensor("bkc", [128, 8], F32, kind="ExternalInput").ap()
    bocH = nc.dram_tensor("boc", [128, 8], F32, kind="ExternalInput").ap()
    b1cH = nc.dram_tensor("b1c", [128, 32], F32, kind="ExternalInput").ap()
    b2cH = nc.dram_tensor("b2c", [128, 8], F32, kind="ExternalInput").ap()
    bvrowH = nc.dram_tensor("bvrow", [1, D], F32, kind="ExternalInput").ap()
    outT = nc.dram_tensor("outT", [D, L], F32, kind="ExternalOutput").ap()
    x1T = nc.dram_tensor("x1T", [D, L], BF16, kind=ikind).ap()
    x2T = nc.dram_tensor("x2T", [D, L], BF16, kind=ikind).ap()
    qTd = nc.dram_tensor("qTd", [D, L], BF16, kind=ikind).ap()

    ctx = ExitStack()
    sb = lambda name, shape, dt: ctx.enter_context(nc.sbuf_tensor(name, shape, dt))
    ps = lambda name: ctx.enter_context(nc.psum_tensor(name, [128, 512], F32))

    AR = sb("arB", [128, 73728], BF16)
    AFr = sb("arF", [128, 6816], F32)
    SQ = [sb("sq0", [128, 512], BF16), sb("sq1", [128, 512], BF16)]
    ET = [sb("et0", [128, 512], BF16), sb("et1", [128, 512], BF16)]
    SQA = sb("sqa", [128, 4096], BF16)
    XTR = sb("xtr", [128, 4096], BF16)
    BVB = sb("bvb", [128, 1024], BF16)
    MNT = sb("mnT", [128, 512], BF16)
    RBT = sb("rbT", [128, 512], BF16)
    SBT = sb("sbT", [128, 512], BF16)
    LNT = sb("lnT", [128, 512], BF16)
    LNT2 = sb("lnT2", [128, 512], BF16)
    ONES = sb("ones", [128, 128], BF16)
    ORF2 = sb("orf2", [128, 256], BF16)
    SEL = sb("sel", [128, 128], BF16)
    BVR = sb("bvr", [1, D], BF16)
    BQC = sb("bqc_s", [128, 8], F32)
    BKC = sb("bkc_s", [128, 8], F32)
    BOC = sb("boc_s", [128, 8], F32)
    KVROW = sb("kvrow", [1, 1040], BF16)
    KROW2 = sb("krow2", [1, 1024], BF16)
    B1C = sb("b1c_s", [128, 32], F32)
    B1B = sb("b1b_s", [128, 32], BF16)
    B2C = sb("b2c_s", [128, 8], F32)
    B2H = sb("b2h_s", [128, 8], F32)
    EPS5 = sb("eps5", [128, 1], F32)
    LNE = sb("lne", [128, 1], F32)

    PB = [ps(f"pb{i}") for i in range(8)]

    semT = Sem(ctx.enter_context(nc.semaphore("semT")))
    semV = Sem(ctx.enter_context(nc.semaphore("semV")))
    semS = Sem(ctx.enter_context(nc.semaphore("semS")))
    semW = Sem(ctx.enter_context(nc.semaphore("semW")))
    semB = Sem(ctx.enter_context(nc.semaphore("semB")))
    semX0 = Sem(ctx.enter_context(nc.semaphore("semX0")))
    semX1 = Sem(ctx.enter_context(nc.semaphore("semX1")))
    semH0 = Sem(ctx.enter_context(nc.semaphore("semH0")))
    semH1 = Sem(ctx.enter_context(nc.semaphore("semH1")))
    semQS = Sem(ctx.enter_context(nc.semaphore("semQS")))
    semQL = Sem(ctx.enter_context(nc.semaphore("semQL")))
    semXB = Sem(ctx.enter_context(nc.semaphore("semXB")))
    semX2S = Sem(ctx.enter_context(nc.semaphore("semX2S")))
    semSC = Sem(ctx.enter_context(nc.semaphore("semSC")))
    semSCg = Sem(ctx.enter_context(nc.semaphore("semSCg")))
    semG = Sem(ctx.enter_context(nc.semaphore("semG")))
    semX0g = Sem(ctx.enter_context(nc.semaphore("semX0g")))
    semX1g = Sem(ctx.enter_context(nc.semaphore("semX1g")))
    semWQ = [Sem(ctx.enter_context(nc.semaphore(f"semWQ{q}"))) for q in range(4)]
    ALLSEMS = [semT, semV, semS, semG, semW, semB, semX0, semX1, semH0, semH1,
               semQS, semQL, semXB, semX2S, semSC, semSCg, semX0g, semX1g] + semWQ

    em = Emitter()
    W, OP, TINC = em.wait, em.op, em.inc_last

    r_w1n, r_w2t, r_wc = 0, 32768, 65536
    r_wq, r_wk, r_wv = 0, 8192, 16384
    r_kc, r_vs, r_kphi = 24576, 28672, 32832
    r_kphi2 = 48272 + 4096
    r_kvbf, r_krep = 33856, 35936
    r_epsp, r_kvsum, r_ksr = 38016, 42112, 43152
    r_qc = 44176
    r_wo, r_e1, r_ap, r_q2 = 0, 8192, 10240, 14336
    f_kvP = [0, 2080]
    f_vsP = [4160, 5200]
    f_recip, f_nnk = 6240, 6752
    f_out = 0

    bank_free = {i: None for i in range(8)}

    def acquire(i):
        W("tensor", bank_free[i])

    qwev = []
    for q in range(4):
        OP("gpsimd", lambda e, q=q: e.dma_start(
            out=AR[:, r_w1n + q * 8192: r_w1n + (q + 1) * 8192]
                .rearrange("p (s d) -> p s d", s=8),
            in_=w1nH.rearrange("(s p) d -> p s d", p=128)[:, q * 8:(q + 1) * 8, :]),
            inc=semWQ[q], k=16)
        qwev.append(OP("gpsimd", lambda e, q=q: e.dma_start(
            out=AR[:, r_w2t + q * 8192: r_w2t + (q + 1) * 8192]
                .rearrange("p (s d) -> p s d", s=8),
            in_=w2tH.rearrange("(s p) d -> p s d", p=128)[:, q * 8:(q + 1) * 8, :]),
            inc=semWQ[q], k=16))
    OP("gpsimd", lambda e: e.dma_start(out=ORF2[0:64, :], in_=orfTH[:, :]), inc=semW, k=16)
    OP("gpsimd", lambda e: e.dma_start(out=ORF2[64:128, :], in_=orfTH[:, :]), inc=semW, k=16)
    ev_const = OP("gpsimd", lambda e: e.dma_start(out=BVR[:, :], in_=bvrowH[:, :]), inc=semW, k=16)
    ev_b = None
    for dst, src in ((BQC, bqcH), (BKC, bkcH), (BOC, bocH), (B1C, b1cH), (B2C, b2cH)):
        ev_b = OP("sync", lambda e, d=dst, s=src: e.dma_start(out=d[:, :], in_=s[:, :]), inc=semB, k=16)

    OP("vector", lambda e: e.memset(ONES[:, :], 1.0))
    v_sel0 = OP("vector", lambda e: e.memset(SEL[:, :], 0.0), inc=semV)
    W("vector", v_sel0)
    for i in range(8):
        OP("vector", lambda e, i=i: e.memset(SEL[0:64, i * 16 + 2 * i: i * 16 + 2 * i + 1], 1.0))
        OP("vector", lambda e, i=i: e.memset(SEL[64:128, i * 16 + 2 * i + 1: i * 16 + 2 * i + 2], 1.0))
    OP("vector", lambda e: e.memset(EPS5[:, :], 1e-5))
    OP("vector", lambda e: e.memset(LNE[:, :], LNEPS))
    W("vector", ev_b)
    ev_b1b = OP("vector", lambda e: e.tensor_copy(B1B[:, :], B1C[:, :]), inc=semV)
    OP("scalar", lambda e: e.add_instruction(mybir.InstLoadActFuncSet(
        name=nc.get_next_instruction_name(), act_func_set_id=6, ins=[], outs=[])))
    ev_wc = None
    for q in range(4):
        W("tensor", qwev[q])
        for ds in range(4):
            for half in range(2):
                bk = ds * 2 + half
                for j in range(q * 8, q * 8 + 8):
                    OP("tensor", lambda e, j=j, ds=ds, half=half, bk=bk: e.matmul(
                        PB[bk][:, :],
                        AR[:, r_w1n + j * 1024 + ds * 128: r_w1n + j * 1024 + (ds + 1) * 128],
                        AR[:, r_w2t + j * 1024 + half * 512: r_w2t + j * 1024 + (half + 1) * 512],
                        start=(j == 0), stop=(j == 31)))
    t_gA = TINC("tensor", semT)
    for ds in range(4):
        for half in range(2):
            bk = ds * 2 + half
            W("scalar", t_gA)
            ev_wc = OP("scalar", lambda e, ds=ds, half=half, bk=bk: e.activation(
                AR[:, r_wc + ds * 1024 + half * 512: r_wc + ds * 1024 + (half + 1) * 512],
                PB[bk][:, :], AF.Copy), inc=semS)
            bank_free[bk] = ev_wc
    for ds in range(4, NS):
        for half in range(2):
            bk = (ds - 4) * 2 + half
            acquire(bk)
            for j in range(32):
                OP("tensor", lambda e, j=j, ds=ds, half=half, bk=bk: e.matmul(
                    PB[bk][:, :],
                    AR[:, r_w1n + j * 1024 + ds * 128: r_w1n + j * 1024 + (ds + 1) * 128],
                    AR[:, r_w2t + j * 1024 + half * 512: r_w2t + j * 1024 + (half + 1) * 512],
                    start=(j == 0), stop=(j == 31)))
            t_g = TINC("tensor", semT)
            W("scalar", t_g)
            ev_wc = OP("scalar", lambda e, ds=ds, half=half, bk=bk: e.activation(
                AR[:, r_wc + ds * 1024 + half * 512: r_wc + ds * 1024 + (half + 1) * 512],
                PB[bk][:, :], AF.Copy), inc=semS)
            bank_free[bk] = ev_wc
    acquire(2)
    W("tensor", ev_b1b)
    for i in range(NS):
        for j in range(32):
            OP("tensor", lambda e, i=i, j=j: e.matmul(
                PB[2][:, i:i + 1],
                AR[:, r_w2t + j * 1024 + i * 128: r_w2t + j * 1024 + (i + 1) * 128],
                B1B[:, j:j + 1], start=(j == 0), stop=(j == 31)))
    t_g = TINC("tensor", semT)
    W("vector", t_g)
    ev_c2h = None
    for i in range(NS):
        ev_c2h = OP("vector", lambda e, i=i: e.tensor_scalar(
            B2H[:, i:i + 1], PB[2][:, i:i + 1], B2C[:, i:i + 1], 0.5,
            ALU.add, ALU.mult), inc=semV)
    bank_free[2] = ev_c2h
    ev_wc_all = t_g

    ln_srb = [None]
    ln_v6 = [None]

    def ln_emit(xbuf, x_evt, outbuf, ev_sqa_free=None, bankA=6, bankB=7):
        W("vector", merge(x_evt, ev_sqa_free, ln_v6[0]))
        v_sq = None
        for s in range(NS):
            xs = xbuf[:, s * CH:(s + 1) * CH]
            v_sq = OP("vector", lambda e, xs=xs, s=s: e.tensor_tensor(
                SQA[:, s * CH:(s + 1) * CH], xs, xs, ALU.mult),
                inc=semV if s == NS - 1 else None)
        acquire(bankA)
        acquire(bankB)
        W("tensor", x_evt)
        for s in range(NS):
            OP("tensor", lambda e, s=s, bA=bankA: e.matmul(
                PB[bA][:, :], ONES[:, :], xbuf[:, s * CH:(s + 1) * CH],
                start=(s == 0), stop=(s == NS - 1)))
        W("tensor", v_sq)
        for s in range(NS):
            OP("tensor", lambda e, s=s, bB=bankB: e.matmul(
                PB[bB][:, :], ONES[:, :], SQA[:, s * CH:(s + 1) * CH],
                start=(s == 0), stop=(s == NS - 1)))
        t_last = TINC("tensor", semT)
        W("vector", t_last)
        v1 = OP("vector", lambda e, bA=bankA: e.tensor_scalar(
            MNT[:, :], PB[bA][:, :], 1.0 / D, None, ALU.mult), inc=semV)
        v2 = OP("vector", lambda e, bB=bankB: e.tensor_scalar(
            SQ[0][:, :], PB[bB][:, :], 1.0 / D, None, ALU.mult), inc=semV)
        bank_free[bankA] = v1
        bank_free[bankB] = v2
        W("vector", v1)
        v3 = OP("vector", lambda e: e.tensor_tensor(
            SQ[1][:, :], MNT[:, :], MNT[:, :], ALU.mult), inc=semV)
        W("vector", v3)
        v4 = OP("vector", lambda e: e.tensor_tensor(
            LNT[:, :], SQ[0][:, :], SQ[1][:, :], ALU.subtract), inc=semV)
        W("scalar", merge(v4, ln_srb[0]))
        s1 = OP("scalar", lambda e: e.activation(
            LNT2[:, :], LNT[:, :], AF.Ln, bias=EPS5[:, 0:1]), inc=semS)
        W("scalar", s1)
        s_rb = OP("scalar", lambda e: e.activation(
            RBT[:, :], LNT2[:, :], AF.Exp, scale=-0.5), inc=semS)
        ln_srb[0] = s_rb
        W("vector", s_rb)
        v5 = OP("vector", lambda e: e.tensor_tensor(
            SBT[:, :], MNT[:, :], RBT[:, :], ALU.mult), inc=semV)
        v6 = None
        for s in range(NS):
            xs = xbuf[:, s * CH:(s + 1) * CH]
            v6 = OP("vector", lambda e, xs=xs, s=s: e.tensor_tensor(
                SQA[:, s * CH:(s + 1) * CH], xs, RBT[:, :], ALU.mult),
                inc=semV if s == NS - 1 else None)
        ln_v6[0] = v6
        W("vector", merge(v5, v6))
        v_ln = None
        for s in range(NS):
            ys = outbuf[:, s * CH:(s + 1) * CH]
            v_ln = OP("vector", lambda e, ys=ys, s=s: e.tensor_tensor(
                ys, SQA[:, s * CH:(s + 1) * CH], SBT[:, :], ALU.subtract),
                inc=semV if s == NS - 1 else None)
        return v_ln

    def ffn_pass(src, src_f32, dst, dst_f32, ev_w, prefetch=None, bufs=0,
                 src_gate=None, pre=None):
        if bufs == 0:
            BX = [AR[:, 24576:28672], AR[:, 28672:32768], AR[:, 45056:49152]]
            BL = [AR[:, 32768:36864], AR[:, 36864:40960], AR[:, 49152:53248]]
            XST = AR[:, 40960:45056]
        else:
            BX = [AR[:, 45056:49152], AR[:, 49152:53248], AR[:, 53248:57344]]
            BL = [AR[:, 57344:61440], AR[:, 61440:65536], AR[:, 32768:36864]]
            XST = AR[:, 24576:28672]
        if pre is not None:
            BX[0] = XTR[:, :]
        gpool = (0, 1, 2, 3, 4, 5) if bufs == 0 else (0, 1, 2, 3)
        nbx = len(BX)
        lq = "gpsimd"

        xsems = [semX0g, semX1g, semSCg]

        def emit_load(c):
            if src_gate is not None:
                W(lq, src_gate(c))
            return OP(lq, lambda e, c=c: e.dma_start(
                out=BX[c % nbx].rearrange("p (s t) -> p s t", s=8),
                in_=src.rearrange("(s p) t -> p s t", p=128)[:, :, c * CH:(c + 1) * CH]),
                inc=xsems[c % 3], k=16)

        def emit_store(c, half):
            if dst_f32:
                stg = AFr[:, f_out + half * 2048: f_out + (half + 1) * 2048]
            else:
                stg = XST[:, half * 2048:(half + 1) * 2048]
            return OP("sync", lambda e, c=c, half=half, stg=stg: e.dma_start(
                out=dst.rearrange("(s p) t -> p s t", p=128)
                    [:, half * 4:(half + 1) * 4, c * CH:(c + 1) * CH],
                in_=stg.rearrange("p (s t) -> p s t", s=4)),
                inc=(semH0 if half == 0 else semH1), k=16)

        lns, t_groups, ev_epi_by = {}, {}, {}
        ev_sthalf = [None, None]
        epi_par = [None, None]
        if pre is None:
            loads = {0: emit_load(0)}
            lns[0] = ln_emit(BX[0], loads[0], BL[0])
            loads[1] = emit_load(1)
            lns[1] = ln_emit(BX[1], loads[1], BL[1])
        else:
            loads = {0: pre[0]}
            lns[0] = pre[1]
            loads[1] = emit_load(1)
        for c in range(NCH):
            ev_ln = lns[c]
            ev_epi = None
            t_g = None
            for i in range(NS):
                bk = gpool[i % len(gpool)]
                acquire(bk)
                W("tensor", merge(ev_ln, ev_w))
                for s in range(NS):
                    OP("tensor", lambda e, s=s, i=i, bk=bk, c=c: e.matmul(
                        PB[bk][:, :],
                        AR[:, r_wc + s * 1024 + i * 128: r_wc + s * 1024 + (i + 1) * 128],
                        BL[c % 3][:, s * CH:(s + 1) * CH], start=(s == 0), stop=(s == NS - 1)))
                t_g = TINC("tensor", semT)
                if i == 0 and c + 2 < NCH:
                    W(lq, ev_epi_by.get(c - 1))
                    loads[c + 2] = emit_load(c + 2)
                if i == 0 and prefetch:
                    prefetch.pop(0)()
                if i == 2 and c == 0 and pre is not None:
                    lns[1] = ln_emit(BX[1], loads[1], BL[1])
                if i == 4 and c + 2 < NCH:
                    W("vector", t_groups.get(c - 1))
                    lns[c + 2] = ln_emit(BX[(c + 2) % nbx], loads[c + 2], BL[(c + 2) % 3])
                W("scalar", merge(t_g, epi_par[i % 2]))
                v_ts = OP("scalar", lambda e, i=i, bk=bk: e.activation(
                    ET[i % 2][:, :], PB[bk][:, :], AF.Identity,
                    bias=B2H[:, i:i + 1], scale=0.5), inc=semS)
                W("vector", merge(v_ts, ev_sthalf[i // 4]))
                if dst_f32:
                    ev_epi = OP("vector", lambda e, i=i, c=c: e.tensor_tensor(
                        AFr[:, f_out + i * CH: f_out + (i + 1) * CH],
                        BX[c % nbx][:, i * CH:(i + 1) * CH], ET[i % 2][:, :], ALU.add), inc=semV)
                else:
                    ev_epi = OP("vector", lambda e, i=i, c=c: e.tensor_tensor(
                        XST[:, i * CH:(i + 1) * CH],
                        BX[c % nbx][:, i * CH:(i + 1) * CH], ET[i % 2][:, :], ALU.add), inc=semV)
                bank_free[bk] = ev_epi
                epi_par[i % 2] = ev_epi
                if i == 3:
                    W("sync", ev_epi)
                    ev_sthalf[0] = emit_store(c, 0)
            W("sync", ev_epi)
            ev_sthalf[1] = emit_store(c, 1)
            ev_epi_by[c] = ev_epi
            t_groups[c] = t_g
        return ev_sthalf[1]

    def _pf(wH, reg, half):
        def go():
            OP("gpsimd", lambda e: e.dma_start(
                out=AR[:, reg + half * 4096:reg + (half + 1) * 4096]
                    .rearrange("p (s d) -> p s d", s=4),
                in_=wH.rearrange("(s p) d -> p s d", p=128)[:, half * 4:(half + 1) * 4, :]),
                inc=semW, k=16)
        return go
    pf = [_pf(wqtH, r_wq, 0), _pf(wqtH, r_wq, 1), _pf(wktH, r_wk, 0),
          _pf(wktH, r_wk, 1), _pf(wvtH, r_wv, 0), _pf(wvtH, r_wv, 1)] + [lambda: None] * 8
    ev_wq = ev_wk = ev_wv = (semW, semW.n + 96)
    ld0 = OP("gpsimd", lambda e: e.dma_start(
        out=XTR[:, :].rearrange("p (s t) -> p s t", s=8),
        in_=x0T.rearrange("(s p) t -> p s t", p=128)[:, :, 0:CH]),
        inc=semX0g, k=16)
    ln0 = ln_emit(XTR[:, :], ld0, AR[:, 32768:36864])
    em.barrier(ALLSEMS)
    ffn_pass(x0T, True, x1T, False, merge(ev_wc, ev_c2h), prefetch=pf,
             pre=(ld0, ln0))
    em.barrier(ALLSEMS)

    OP("vector", lambda e: e.memset(
        AR[:, r_vs:r_vs + 4160].rearrange("p (a c) -> p a c", c=65)[:, :, 64:65], 1.0))
    OP("vector", lambda e: e.memset(AR[0:16, r_kvsum:r_kvsum + 1040], 0.0))
    OP("vector", lambda e: e.memset(AR[0:16, r_ksr:r_ksr + 1024], 0.0))
    OP("vector", lambda e: e.memset(AFr[:, f_kvP[0]:f_kvP[0] + 2080], 0.0))
    OP("vector", lambda e: e.memset(AFr[:, f_vsP[0]:f_vsP[0] + 1040], 0.0))

    def qkv_pass(wreg, wevt, bcol, dst, xlnb, ev_ln, ev_dst_free, sel_mode):
        ev_ev = None
        v_sq = None
        pool = (0, 1, 4, 5)
        for i in range(NS):
            bk = pool[i % 4]
            acquire(bk)
            W("tensor", merge(ev_ln, wevt))
            for s in range(NS):
                OP("tensor", lambda e, s=s, i=i, bk=bk, wreg=wreg, xlnb=xlnb: e.matmul(
                    PB[bk][:, :],
                    AR[:, wreg + s * 1024 + i * 128: wreg + s * 1024 + (i + 1) * 128],
                    xlnb[:, s * CH:(s + 1) * CH], start=(s == 0), stop=(s == NS - 1)))
            t_g = TINC("tensor", semT)
            W("scalar", merge(t_g, ev_dst_free))
            ev_ev = OP("scalar", lambda e, i=i, bk=bk, dst=dst, bcol=bcol: e.activation(
                AR[:, dst + i * CH: dst + (i + 1) * CH], PB[bk][:, :],
                AF.Identity, bias=bcol[:, i:i + 1]), inc=semS)
            bank_free[bk] = ev_ev
            W("vector", ev_ev)
            v_sq = OP("vector", lambda e, i=i, dst=dst: e.tensor_tensor(
                SQA[:, i * CH:(i + 1) * CH], AR[:, dst + i * CH: dst + (i + 1) * CH],
                AR[:, dst + i * CH: dst + (i + 1) * CH], ALU.mult),
                inc=semV if i == NS - 1 else None)
        acquire(2 if sel_mode == "q" else 3)
        W("tensor", v_sq)
        if sel_mode == "q":
            for i in range(NS):
                OP("tensor", lambda e, i=i: e.matmul(
                    PB[2][0:16, :], SEL[:, i * 16:(i + 1) * 16], SQA[:, i * CH:(i + 1) * CH],
                    start=(i == 0), stop=(i == NS - 1)))
        else:
            for tt in range(4):
                for i in range(NS):
                    OP("tensor", lambda e, i=i, tt=tt: e.matmul(
                        PB[3][:, tt * 16:(tt + 1) * 16],
                        SQA[:, i * CH + tt * 128: i * CH + (tt + 1) * 128],
                        SEL[:, i * 16:(i + 1) * 16],
                        start=(i == 0), stop=(i == NS - 1)))
        ev_sel = TINC("tensor", semT)
        return ev_ev, ev_sel

    acquire(0)
    acquire(1)
    W("tensor", ev_const)
    OP("tensor", lambda e: e.matmul(PB[0][:, 0:512], ONES[0:1, :], BVR[0:1, 0:512],
                                    start=True, stop=True))
    OP("tensor", lambda e: e.matmul(PB[1][:, 0:512], ONES[0:1, :], BVR[0:1, 512:1024],
                                    start=True, stop=True))
    t_bvb = TINC("tensor", semT)
    W("vector", t_bvb)
    OP("vector", lambda e: e.tensor_copy(BVB[:, 0:512], PB[0][:, :]))
    ev_bvb = OP("vector", lambda e: e.tensor_copy(BVB[:, 512:1024], PB[1][:, :]), inc=semV)
    bank_free[0] = ev_bvb
    bank_free[1] = ev_bvb

    aXL = [AR[:, 53392:57488], AR[:, 57488:61584]]
    aX0 = AR[:, 48272:52368]
    ev_q_stored = None
    ev_kc_free = None
    ev_vs_free = None
    ev_kv_add = None
    ev_vsum_add = None
    ev_epsp = None
    ev_kphi_used = [None, None]
    t_kv = None
    t_vlast = {}
    lns2 = {}
    loads2 = {}

    def emit_xload2(c):
        return OP("sync", lambda e, c=c: e.dma_start(
            out=aX0.rearrange("p (s t) -> p s t", s=8),
            in_=x1T.rearrange("(s p) t -> p s t", p=128)[:, :, c * CH:(c + 1) * CH]),
            inc=(semX0 if c % 2 == 0 else semX1), k=16)

    loads2[0] = emit_xload2(0)
    lns2[0] = ln_emit(aX0, loads2[0], aXL[0])
    for c in range(NCH):
        ev_ln = lns2[c]
        XL = aXL[c % 2]
        if c + 1 < NCH:
            W("sync", ev_ln)
            loads2[c + 1] = emit_xload2(c + 1)
        ev_qev, ev_selq = qkv_pass(r_wq, ev_wq, BQC, r_qc, XL, ev_ln, ev_q_stored, "q")
        W("scalar", ev_selq)
        ev_epsp = OP("scalar", lambda e, c=c: e.activation(
            AR[0:16, r_epsp + c * CH: r_epsp + (c + 1) * CH], PB[2][0:16, :],
            AF.Exp, bias=LNE[0:16, 0:1], scale=1.0 / 16.0), inc=semS)
        bank_free[2] = ev_epsp
        W("sync", ev_qev)
        ev_q_stored = OP("sync", lambda e, c=c: e.dma_start(
            out=qTd.rearrange("(s p) t -> p s t", p=128)[:, :, c * CH:(c + 1) * CH],
            in_=AR[:, r_qc:r_qc + 4096].rearrange("p (s t) -> p s t", s=8)), inc=semQS, k=16)
        ev_kev, ev_selk = qkv_pass(r_wk, ev_wk, BKC, r_kc, XL, ev_ln, ev_kc_free, "k")
        W("vector", ev_selk)
        ev_nnk = OP("vector", lambda e: e.tensor_scalar(
            AFr[:, f_nnk:f_nnk + 64], PB[3][:, 0:64], -1.0 / 16.0, None, ALU.mult), inc=semV)
        bank_free[3] = ev_nnk
        if c + 1 < NCH:
            W("vector", t_vlast.get(c - 1))
            lns2[c + 1] = ln_emit(aX0, loads2[c + 1], aXL[(c + 1) % 2],
                                  ev_sqa_free=ev_selk)
        ev_vev = None
        t_g = None
        vpool = (0, 1, 4, 5)
        for tt in range(4):
            for hf in range(2):
                bk = vpool[(tt * 2 + hf) % 4]
                acquire(bk)
                W("tensor", merge(ev_ln, ev_wv, ev_vs_free))
                for s in range(NS):
                    OP("tensor", lambda e, s=s, tt=tt, hf=hf, bk=bk, XL=XL: e.matmul(
                        PB[bk][:, :],
                        XL[:, s * CH + tt * 128: s * CH + (tt + 1) * 128],
                        AR[:, r_wv + s * 1024 + hf * 512: r_wv + s * 1024 + (hf + 1) * 512],
                        start=(s == 0), stop=(s == NS - 1)))
                t_g = TINC("tensor", semT)
                W("vector", merge(t_g, ev_bvb))
                ev_vev = OP("vector", lambda e, tt=tt, hf=hf, bk=bk: e.tensor_tensor(
                    AR[:, r_vs + tt * 1040 + hf * 520: r_vs + tt * 1040 + hf * 520 + 520]
                    .rearrange("p (h c) -> p h c", c=65)[:, :, 0:64],
                    PB[bk][:, :].rearrange("p (h c) -> p h c", c=64),
                    BVB[:, hf * 512:(hf + 1) * 512].rearrange("p (h c) -> p h c", c=64),
                    ALU.add), inc=semV)
                bank_free[bk] = ev_vev
        t_vlast[c] = t_g
        kv_old = f_kvP[c % 2]
        kv_new = f_kvP[(c + 1) % 2]
        units = [(h, tt) for h in range(H) for tt in range(4)]
        projd, expd = {}, {}

        def emit_proj(k):
            h, tt = units[k]
            prow0 = (h % 2) * 64
            pkb = 4 + (k % 4)
            acquire(pkb)
            W("tensor", merge(ev_kev, ev_const))
            OP("tensor", lambda e, h=h, tt=tt, pkb=pkb, prow0=prow0: e.matmul(
                PB[pkb][:, 0:256],
                AR[prow0:prow0 + 64,
                   r_kc + (h // 2) * 512 + tt * 128: r_kc + (h // 2) * 512 + (tt + 1) * 128],
                ORF2[prow0:prow0 + 64, :], start=True, stop=True))
            projd[k] = (TINC("tensor", semT), pkb)

        def emit_exp(k):
            h, tt = units[k]
            kp = r_kphi if h % 2 == 0 else r_kphi2
            t_g, pkb = projd.pop(k)
            W("scalar", merge(t_g, ev_nnk, ev_kphi_used[h % 2]))
            s_exp = OP("scalar", lambda e, h=h, tt=tt, pkb=pkb, kp=kp: e.activation(
                AR[:, kp + tt * 256: kp + (tt + 1) * 256], PB[pkb][:, 0:256],
                AF.Exp, bias=AFr[:, f_nnk + tt * 16 + h: f_nnk + tt * 16 + h + 1],
                scale=PSCALE), inc=semS)
            bank_free[pkb] = s_exp
            expd[k] = s_exp

        def emit_kv(k):
            nonlocal t_kv, ev_kv_add
            h, tt = units[k]
            kvbks = (0, 1) if h % 2 == 0 else (2, 3)
            kp = r_kphi if h % 2 == 0 else r_kphi2
            if tt == 0:
                acquire(kvbks[0])
                acquire(kvbks[1])
            W("tensor", merge(expd.pop(k), ev_vev, ev_kv_add))
            for mh in range(2):
                OP("tensor", lambda e, h=h, tt=tt, mh=mh, bk=kvbks[mh], kp=kp: e.matmul(
                    PB[bk][:, 0:65],
                    AR[:, kp + tt * 256 + mh * 128: kp + tt * 256 + (mh + 1) * 128],
                    AR[:, r_vs + tt * 1040 + h * 65: r_vs + tt * 1040 + h * 65 + 65],
                    start=(tt == 0), stop=(tt == 3)))
            if tt == 3:
                t_kv = TINC("tensor", semT)
                ev_kphi_used[h % 2] = t_kv
                W("vector", t_kv)
                for mh in range(2):
                    ev_kv_add = OP("vector", lambda e, h=h, mh=mh, bk=kvbks[mh],
                                   ko=kv_old, kn=kv_new: e.tensor_tensor(
                        AFr[:, kn + h * 130 + mh * 65: kn + h * 130 + (mh + 1) * 65],
                        AFr[:, ko + h * 130 + mh * 65: ko + h * 130 + (mh + 1) * 65],
                        PB[bk][:, 0:65], ALU.add), inc=semV)
                    bank_free[kvbks[mh]] = ev_kv_add

        for k in range(len(units)):
            emit_proj(k)
            if k >= 1:
                emit_exp(k - 1)
            if k >= 3:
                emit_kv(k - 3)
        emit_exp(len(units) - 1)
        emit_kv(len(units) - 3)
        emit_kv(len(units) - 2)
        emit_kv(len(units) - 1)
        vs_old = f_vsP[c % 2]
        vs_new = f_vsP[(c + 1) % 2]
        for vj, (off, n) in enumerate(((0, 512), (512, 512), (1024, 16))):
            vb = 6 + (vj % 2)
            acquire(vb)
            W("tensor", ev_vev)
            for tt in range(4):
                OP("tensor", lambda e, tt=tt, off=off, n=n, vb=vb: e.matmul(
                    PB[vb][:, 0:n], ONES[:, :],
                    AR[:, r_vs + tt * 1040 + off: r_vs + tt * 1040 + off + n],
                    start=(tt == 0), stop=(tt == 3)))
            t_g = TINC("tensor", semT)
            W("vector", t_g)
            ev_vsum_add = OP("vector", lambda e, off=off, n=n, vo=vs_old, vn=vs_new, vb=vb: e.tensor_tensor(
                AFr[:, vn + off: vn + off + n],
                AFr[:, vo + off: vo + off + n], PB[vb][:, 0:n], ALU.add), inc=semV)
            bank_free[vb] = ev_vsum_add
        ev_kc_free = t_kv
        ev_vs_free = t_kv
    W("gpsimd", ev_qev)
    ev_wo = OP("gpsimd", lambda e: e.dma_start(
        out=AR[:, r_wo:r_wo + 8192].rearrange("p (s d) -> p s d", s=8),
        in_=wotH.rearrange("(s p) d -> p s d", p=128)), inc=semW, k=16)
    W("vector", merge(ev_vsum_add, ev_kv_add))
    v_eps = OP("vector", lambda e: e.tensor_scalar(
        AFr[:, f_vsP[1]:f_vsP[1] + 1040], AFr[:, f_vsP[0]:f_vsP[0] + 1040],
        1e-6, None, ALU.mult), inc=semV)
    W("vector", v_eps)
    ev_kvbf = None
    for h in range(H):
        for mh in range(2):
            ev_kvbf = OP("vector", lambda e, h=h, mh=mh: e.tensor_tensor(
                AR[:, r_kvbf + h * 130 + mh * 65: r_kvbf + h * 130 + (mh + 1) * 65],
                AFr[:, f_kvP[0] + h * 130 + mh * 65: f_kvP[0] + h * 130 + (mh + 1) * 65],
                AFr[:, f_vsP[1] + h * 65: f_vsP[1] + (h + 1) * 65], ALU.add), inc=semV)
    ev_rows = None
    for h in range(H):
        bk = h % 2
        acquire(bk)
        W("tensor", ev_kvbf)
        for mh in range(2):
            OP("tensor", lambda e, h=h, mh=mh, bk=bk: e.matmul(
                PB[bk][0:1, 0:64], ONES[:, 0:1],
                AR[:, r_kvbf + h * 130 + mh * 65: r_kvbf + h * 130 + mh * 65 + 64],
                start=(mh == 0), stop=(mh == 1)))
        for mh in range(2):
            OP("tensor", lambda e, h=h, mh=mh, bk=bk: e.matmul(
                PB[bk][0:1, 496:497], ONES[:, 0:1],
                AR[:, r_kvbf + h * 130 + mh * 65 + 64: r_kvbf + h * 130 + mh * 65 + 65],
                start=(mh == 0), stop=(mh == 1)))
        t_g = TINC("tensor", semT)
        W("vector", t_g)
        OP("vector", lambda e, h=h, bk=bk: e.tensor_copy(
            KVROW[0:1, h * 65: h * 65 + 64], PB[bk][0:1, 0:64]))
        ev_kvr = OP("vector", lambda e, h=h, bk=bk: e.tensor_copy(
            KVROW[0:1, h * 65 + 64: h * 65 + 65], PB[bk][0:1, 496:497]), inc=semV)
        W("tensor", ev_kvr)
        OP("tensor", lambda e, h=h, bk=bk: e.matmul(
            PB[bk][0:1, 64:128], KVROW[0:1, h * 65 + 64: h * 65 + 65],
            ONES[0:1, 0:64], start=True, stop=True))
        t_g2 = TINC("tensor", semT)
        W("vector", t_g2)
        ev_rows = OP("vector", lambda e, h=h, bk=bk: e.tensor_copy(
            KROW2[0:1, h * 64: (h + 1) * 64], PB[bk][0:1, 64:128]), inc=semV)
        bank_free[bk] = ev_rows
    ev_scat = None
    for h in range(H):
        q = "sync" if h % 2 == 0 else "gpsimd"
        qs2 = semSC if h % 2 == 0 else semSCg
        W(q, ev_rows)
        OP(q, lambda e, h=h: e.dma_start(
            out=AR[h:h + 1, r_kvsum + h * 65: r_kvsum + (h + 1) * 65],
            in_=KVROW[0:1, h * 65: (h + 1) * 65]), inc=qs2, k=16)
        ev_scat = OP(q, lambda e, h=h: e.dma_start(
            out=AR[h:h + 1, r_ksr + h * 64: r_ksr + (h + 1) * 64],
            in_=KROW2[0:1, h * 64: (h + 1) * 64]), inc=qs2, k=16)
    ev_krep = None
    for h in range(H):
        for mh in range(2):
            ev_krep = OP("vector", lambda e, h=h, mh=mh: e.tensor_scalar(
                AR[:, r_krep + (2 * h + mh) * 64: r_krep + (2 * h + mh + 1) * 64],
                ONES[:, 0:64],
                AFr[:, f_kvP[0] + h * 130 + mh * 65 + 64: f_kvP[0] + h * 130 + mh * 65 + 65],
                float(4096 * 1e-6), ALU.mult, ALU.add), inc=semV)
    ev_p2a_t = (semT, semT.n)
    ev_scat_tot = {semSC: semSC.n, semSCg: semSCg.n}
    ev_qs_tot = (semQS, semQS.n)
    if dbg:
        d_kvbf = nc.dram_tensor("d_kvbf", [128, 2080], BF16, kind="ExternalOutput").ap()
        d_kvsb = nc.dram_tensor("d_kvsb", [128, 2080], F32, kind="ExternalOutput").ap()
        d_vsum = nc.dram_tensor("d_vsum", [128, 1040], F32, kind="ExternalOutput").ap()
        d_krep = nc.dram_tensor("d_krep", [128, 2048], BF16, kind="ExternalOutput").ap()
        d_epsp = nc.dram_tensor("d_epsp", [16, 4096], BF16, kind="ExternalOutput").ap()
        d_kvsum = nc.dram_tensor("d_kvsum", [16, 1040], BF16, kind="ExternalOutput").ap()
        d_ksr = nc.dram_tensor("d_ksr", [16, 1024], BF16, kind="ExternalOutput").ap()
        d_krow2 = nc.dram_tensor("d_krow2", [1, 1024], BF16, kind="ExternalOutput").ap()
        OP("sync", lambda e: e.dma_start(out=d_krow2[:, :], in_=KROW2[:, :]), inc=semSC, k=16)
        OP("sync", lambda e: e.dma_start(out=d_kvbf[:, :], in_=AR[:, r_kvbf:r_kvbf + 2080]), inc=semSC, k=16)
        OP("sync", lambda e: e.dma_start(out=d_kvsb[:, :], in_=AFr[:, f_kvP[0]:f_kvP[0] + 2080]), inc=semSC, k=16)
        OP("sync", lambda e: e.dma_start(out=d_vsum[:, :], in_=AFr[:, f_vsP[1]:f_vsP[1] + 1040]), inc=semSC, k=16)
        OP("sync", lambda e: e.dma_start(out=d_krep[:, :], in_=AR[:, r_krep:r_krep + 2048]), inc=semSC, k=16)
        OP("sync", lambda e: e.dma_start(out=d_epsp[:, :], in_=AR[0:16, r_epsp:r_epsp + 4096]), inc=semSC, k=16)
        OP("sync", lambda e: e.dma_start(out=d_kvsum[:, :], in_=AR[0:16, r_kvsum:r_kvsum + 1040]), inc=semSC, k=16)
        OP("sync", lambda e: e.dma_start(out=d_ksr[:, :], in_=AR[0:16, r_ksr:r_ksr + 1024]), inc=semSC, k=16)
        em.barrier(ALLSEMS)

    ev_ap_tread = None
    ev_x2st = None
    ev_epi2 = None
    qloads2, xloads2 = {}, {}

    q2buf = [AR[:, r_q2:r_q2 + 4096], AR[:, 18432:22528]]

    def emit_qload3(c):
        return OP("sync", lambda e, c=c: e.dma_start(
            out=q2buf[c % 2].rearrange("p (s t) -> p s t", s=8),
            in_=qTd.rearrange("(s p) t -> p s t", p=128)[:, :, c * CH:(c + 1) * CH]),
            inc=semQL, k=16)

    bX0 = AR[:, 24576:28672]
    bXST = AR[:, 28672:32768]

    def emit_xload3(c):
        return OP("sync", lambda e, c=c: e.dma_start(
            out=bX0.rearrange("p (s t) -> p s t", s=8),
            in_=x1T.rearrange("(s p) t -> p s t", p=128)[:, :, c * CH:(c + 1) * CH]),
            inc=semXB, k=16)

    tnd_by = {}
    W("sync", merge(ev_p2a_t, ev_qs_tot))
    qloads2[0] = emit_qload3(0)
    xloads2[0] = emit_xload3(0)
    for c in range(NCH):
        if c + 1 < NCH:
            W("sync", tnd_by.get(c - 1))
            qloads2[c + 1] = emit_qload3(c + 1)
        ev_qload = qloads2[c]
        ev_xload = xloads2[c]
        ev_e1_used = [None, None]
        t_nd = None
        ev_pair = None
        exps_by = {}
        pair_by = {}

        def emit_proj(h, ev_qload=ev_qload):
            prow0 = (h % 2) * 64
            s_exps = []
            for ms in range(2):
                pqb = (h % 2) * 6 + ms
                acquire(pqb)
                W("tensor", merge(ev_qload, ev_const))
                OP("tensor", lambda e, h=h, ms=ms, pqb=pqb, prow0=prow0, c=c: e.matmul(
                    PB[pqb][:, :], ORF2[prow0:prow0 + 64, ms * 128:(ms + 1) * 128],
                    q2buf[c % 2][prow0:prow0 + 64, (h // 2) * 512: (h // 2 + 1) * 512],
                    start=True, stop=True))
                t_g = TINC("tensor", semT)
                W("scalar", merge(t_g, ev_e1_used[h % 2]))
                s_exp = OP("scalar", lambda e, h=h, ms=ms, pqb=pqb: e.activation(
                    AR[:, r_e1 + (h % 2) * 1024 + ms * 512: r_e1 + (h % 2) * 1024 + (ms + 1) * 512],
                    PB[pqb][:, :], AF.Exp, scale=PSCALE), inc=semS)
                s_exps.append(s_exp)
                bank_free[pqb] = s_exp
            exps_by[h] = s_exps

        emit_proj(0)
        for h in range(H):
            if h + 1 < H:
                emit_proj(h + 1)
            prow0 = (h % 2) * 64
            numbk = 2 + ((h // 2) % 2)
            denbk = 4 + ((h // 2) % 2)
            s_exps = exps_by.pop(h)
            if h % 2 == 0:
                acquire(numbk)
                acquire(denbk)
            W("tensor", merge(s_exps[0], s_exps[1], ev_kvbf, ev_krep, ev_epsp,
                              ev_scat_tot))
            e1b = r_e1 + (h % 2) * 1024
            OP("tensor", lambda e, h=h, numbk=numbk, e1b=e1b, prow0=prow0: e.matmul(
                PB[numbk][prow0:prow0 + 64, :],
                AR[:, r_kvbf + h * 130: r_kvbf + h * 130 + 64],
                AR[:, e1b:e1b + 512], start=True, stop=False))
            OP("tensor", lambda e, h=h, numbk=numbk, e1b=e1b, prow0=prow0: e.matmul(
                PB[numbk][prow0:prow0 + 64, :],
                AR[:, r_kvbf + h * 130 + 65: r_kvbf + h * 130 + 65 + 64],
                AR[:, e1b + 512:e1b + 1024], start=False, stop=False))
            OP("tensor", lambda e, h=h, c=c, numbk=numbk, prow0=prow0: e.matmul(
                PB[numbk][prow0:prow0 + 64, :],
                AR[0:16, r_kvsum + h * 65: r_kvsum + h * 65 + 64],
                AR[0:16, r_epsp + c * CH: r_epsp + (c + 1) * CH],
                start=False, stop=True))
            OP("tensor", lambda e, h=h, denbk=denbk, e1b=e1b, prow0=prow0: e.matmul(
                PB[denbk][prow0:prow0 + 64, :],
                AR[:, r_krep + 2 * h * 64: r_krep + (2 * h + 1) * 64],
                AR[:, e1b:e1b + 512], start=True, stop=False))
            OP("tensor", lambda e, h=h, denbk=denbk, e1b=e1b, prow0=prow0: e.matmul(
                PB[denbk][prow0:prow0 + 64, :],
                AR[:, r_krep + (2 * h + 1) * 64: r_krep + (2 * h + 2) * 64],
                AR[:, e1b + 512:e1b + 1024], start=False, stop=False))
            OP("tensor", lambda e, h=h, c=c, denbk=denbk, prow0=prow0: e.matmul(
                PB[denbk][prow0:prow0 + 64, :],
                AR[0:16, r_ksr + h * 64: r_ksr + (h + 1) * 64],
                AR[0:16, r_epsp + c * CH: r_epsp + (c + 1) * CH],
                start=False, stop=True))
            t_nd = TINC("tensor", semT)
            ev_e1_used[h % 2] = t_nd
            if h % 2 == 1:
                W("vector", merge(t_nd, ev_ap_tread, ev_pair))
                v_rec = OP("vector", lambda e, denbk=denbk: e.reciprocal(
                    AFr[:, f_recip:f_recip + 512], PB[denbk][:, :]), inc=semV)
                W("vector", v_rec)
                ev_pair = OP("vector", lambda e, h=h, numbk=numbk: e.tensor_tensor(
                    AR[:, r_ap + (h // 2) * 512: r_ap + (h // 2 + 1) * 512],
                    PB[numbk][:, :], AFr[:, f_recip:f_recip + 512], ALU.mult), inc=semV)
                pair_by[h // 2] = ev_pair
                bank_free[numbk] = ev_pair
                bank_free[denbk] = ev_pair
        tnd_by[c] = t_nd
        t_po_last = None
        for i in range(NS):
            bk = 6 + (i % 2)
            acquire(bk)
            W("tensor", merge(ev_pair, ev_wo))
            for s in range(NS):
                OP("tensor", lambda e, s=s, i=i, bk=bk: e.matmul(
                    PB[bk][:, :],
                    AR[:, r_wo + s * 1024 + i * 128: r_wo + s * 1024 + (i + 1) * 128],
                    AR[:, r_ap + s * CH: r_ap + (s + 1) * CH], start=(s == 0), stop=(s == NS - 1)))
            t_g = TINC("tensor", semT)
            t_po_last = t_g
            W("scalar", merge(t_g, ev_epi2))
            tmp = RBT if i % 2 == 0 else SBT
            s_tmp = OP("scalar", lambda e, i=i, bk=bk, tmp=tmp: e.activation(
                tmp[:, :], PB[bk][:, :], AF.Identity, bias=BOC[:, i:i + 1]), inc=semS)
            W("vector", merge(s_tmp, ev_xload, ev_x2st))
            ev_epi2 = OP("vector", lambda e, i=i, tmp=tmp: e.tensor_tensor(
                bXST[:, i * CH:(i + 1) * CH], bX0[:, i * CH:(i + 1) * CH],
                tmp[:, :], ALU.add), inc=semV)
            bank_free[bk] = ev_epi2
        if c + 1 < NCH:
            W("sync", ev_epi2)
            xloads2[c + 1] = emit_xload3(c + 1)
        W("sync", ev_epi2)
        ev_x2st = OP("sync", lambda e, c=c: e.dma_start(
            out=x2T.rearrange("(s p) t -> p s t", p=128)[:, :, c * CH:(c + 1) * CH],
            in_=bXST.rearrange("p (s t) -> p s t", s=8)), inc=semX2S, k=16)
        ev_ap_tread = t_po_last

    em.wait("scalar", ev_epi2)
    em.wait("vector", ev_epi2)
    ffn_pass(x2T, False, outT, True, None, bufs=1,
             src_gate=lambda c: (semX2S, 16 * (c + 1)))

    em.wait("sync", {s: s.n for s in (semW, semB, semX0, semX1, semH0, semH1,
                                       semQS, semQL, semXB, semX2S, semSC, semSCg,
                                       semX0g, semX1g)})

    with nc.Block() as block:
        @block.tensor
        def _(e):
            for f in em.streams["tensor"]:
                f(e)

        @block.vector
        def _(e):
            for f in em.streams["vector"]:
                f(e)

        @block.scalar
        def _(e):
            for f in em.streams["scalar"]:
                f(e)

        @block.gpsimd
        def _(e):
            for f in em.streams["gpsimd"]:
                f(e)

        @block.sync
        def _(e):
            for f in em.streams["sync"]:
                f(e)

    ctx.close()
    return nc


def kernel(L0_attn_x, L0_attn_Wq, L0_attn_bq, L0_attn_Wk, L0_attn_bk,
           L0_attn_Wv, L0_attn_bv, L0_attn_Wo, L0_attn_bo, L0_attn_orf,
           L0_ffn_W1, L0_ffn_b1, L0_ffn_W2, L0_ffn_b2):
    f = np.float32
    x = np.asarray(L0_attn_x, f)
    Wq, Wk, Wv, Wo = (np.asarray(a, f) for a in (L0_attn_Wq, L0_attn_Wk, L0_attn_Wv, L0_attn_Wo))
    bq, bk, bv, bo = (np.asarray(a, f) for a in (L0_attn_bq, L0_attn_bk, L0_attn_bv, L0_attn_bo))
    orf = np.asarray(L0_attn_orf, f)
    W1, b1, W2, b2 = (np.asarray(a, f) for a in (L0_ffn_W1, L0_ffn_b1, L0_ffn_W2, L0_ffn_b2))

    col = lambda b: np.ascontiguousarray(b.reshape(-1, 128).T)
    wqT = np.ascontiguousarray(Wq.T)
    wkT = np.ascontiguousarray(Wk.T)
    wvT = np.ascontiguousarray(Wv.T)
    woT = np.ascontiguousarray(Wo.T)
    orfT = np.ascontiguousarray(orf.T)
    in_maps = []
    for c in range(8):
        in_maps.append({
            "x0T": np.ascontiguousarray(x[c].T),
            "w1n": np.ascontiguousarray(W1[c]),
            "w2t": np.ascontiguousarray(W2[c].T),
            "wqt": wqT, "wkt": wkT, "wvt": wvT, "wot": woT,
            "orfT": orfT,
            "bqc": col(bq), "bkc": col(bk), "boc": col(bo),
            "b1c": col(b1[c]), "b2c": col(b2[c]),
            "bvrow": np.ascontiguousarray(bv.reshape(1, -1)),
        })
    nc = build()
    res = run_bass_kernel_spmd(nc, in_maps, core_ids=list(range(8)))
    out = np.stack([np.ascontiguousarray(np.asarray(r["outT"]).T) for r in res.results])
    return out.astype(np.float32)
